# version 09
# baseline (speedup 1.0000x reference)
import numpy as np
import concourse.bass as bass
import concourse.mybir as mybir
from concourse.bass_utils import run_bass_kernel_spmd
from contextlib import ExitStack

F32 = mybir.dt.float32
BF16 = mybir.dt.bfloat16
AF = mybir.ActivationFunctionType
ALU = mybir.AluOpType

B = 4
T = 4096
D = 2048
N = 16
RK = 64
E = 96
II = 5632
TC = 512
NCH = T // TC
DJ = D // 2
KD = DJ // 128
KF = D // 128
NIT = II // 128
OWN = T // 2
NMEG = 2
MW = 1024
EPS = 1e-5

ENGINES = ("sync", "tensor", "scalar", "vector", "gpsimd")


class Prog:

    def __init__(self):
        self.streams = {e: [] for e in ENGINES}
        self.counts = {}

    def sem(self, name):
        self.counts[name] = 0
        return name

    def op(self, eng, thunk, inc=None, inc_amt=1):
        if inc is not None:
            self.counts[inc] += inc_amt

            def wrapped(e, S, thunk=thunk, inc=inc, amt=inc_amt):
                thunk(e, S).then_inc(S[inc], amt)
            self.streams[eng].append(wrapped)
        else:
            self.streams[eng].append(lambda e, S, thunk=thunk: thunk(e, S))
        return self.counts.get(inc)

    def wait(self, eng, sem, val=None):
        if val is None:
            val = self.counts[sem]
        if val <= 0:
            return

        def w(e, S, sem=sem, val=val):
            e.wait_ge(S[sem], val)
        self.streams[eng].append(w)

    def cnt(self, sem):
        return self.counts[sem]


def build(upto=5):
    nc = bass.Bass("TRN2", target_bir_lowering=False)
    XT = nc.dram_tensor("XT", [D, T], F32, kind="ExternalInput").ap()
    XSC = nc.dram_tensor("XSC", [DJ, T], F32, kind="ExternalInput").ap()
    XOWN = nc.dram_tensor("XOWN", [D, OWN], F32, kind="ExternalInput").ap()
    WXT = nc.dram_tensor("WXT", [D, E], F32, kind="ExternalInput").ap()
    WDTT = nc.dram_tensor("WDTT", [RK, DJ], F32, kind="ExternalInput").ap()
    ALR = nc.dram_tensor("ALR", [128, KD * N], F32, kind="ExternalInput").ap()
    BDT = nc.dram_tensor("BDT", [128, KD], F32, kind="ExternalInput").ap()
    DV = nc.dram_tensor("DV", [128, KF], F32, kind="ExternalInput").ap()
    MK = nc.dram_tensor("MK", [128, 2], F32, kind="ExternalInput").ap()
    GWT = nc.dram_tensor("GWT", [D, II], F32, kind="ExternalInput").ap()
    UWT = nc.dram_tensor("UWT", [D, II], F32, kind="ExternalInput").ap()
    DWT = nc.dram_tensor("DWT", [II, D], F32, kind="ExternalInput").ap()
    GS = nc.dram_tensor("GS", [128, NIT], F32, kind="ExternalInput").ap()
    US = nc.dram_tensor("US", [128, NIT], F32, kind="ExternalInput").ap()
    DS = nc.dram_tensor("DS", [128, KF], F32, kind="ExternalInput").ap()
    SEL = nc.dram_tensor("SEL", [N, N * 128], F32, kind="ExternalInput").ap()
    IDN = nc.dram_tensor("IDN", [128, 128], F32, kind="ExternalInput").ap()
    OUT = nc.dram_tensor("OUT", [OWN, D], F32, kind="ExternalOutput").ap()
    DBGW = nc.dram_tensor("DBGW", [128, 1536], F32, kind="ExternalOutput").ap()

    CCI = [nc.dram_tensor(f"CCI{p}", [DJ, TC], BF16).ap() for p in range(2)]
    CCO = [nc.dram_tensor(f"CCO{p}", [D, TC], BF16).ap() for p in range(2)]
    X1D = nc.dram_tensor("X1D", [D, OWN], BF16).ap()
    GWB = nc.dram_tensor("GWB", [D, II], BF16).ap()
    UWB = nc.dram_tensor("UWB", [D, II], BF16).ap()
    DWB = nc.dram_tensor("DWB", [II, D], BF16).ap()

    P = Prog()
    LD = P.sem("LD")
    GD = P.sem("GD")
    PE = P.sem("PE")
    AC = P.sem("AC")
    VE = P.sem("VE")
    PO = P.sem("PO")
    CC = P.sem("CC")
    WB = P.sem("WB")
    CL = [P.sem(f"CL{i}") for i in range(2)]
    GL = [P.sem(f"GL{i}") for i in range(4)]
    UL = [P.sem(f"UL{i}") for i in range(4)]
    DL = [P.sem(f"DL{i}") for i in range(4)]
    YG0 = [P.sem(f"YG0{i}") for i in range(2)]
    YG1 = [P.sem(f"YG1{i}") for i in range(2)]
    XRs = [P.sem(f"XRs{i}") for i in range(2)]
    XS = [P.sem(f"XS{i}") for i in range(2)]
    XF = [P.sem(f"XF{i}") for i in range(2)]
    X1S = P.sem("X1S")
    OS = P.sem("OS")

    def ld2(dst_ap, src_ap, sem):
        return P.op("sync", lambda e, S, d=dst_ap, s=src_ap: e.dma_start(out=d, in_=s),
                    inc=sem, inc_amt=16)

    def ld(dst_ap, src_ap):
        return P.op("sync", lambda e, S, d=dst_ap, s=src_ap: e.dma_start(out=d, in_=s),
                    inc=LD, inc_amt=16)

    stTop = ExitStack()
    sb = lambda name, shp, dt=F32: stTop.enter_context(nc.sbuf_tensor(name, shp, dt))

    with stTop:
        s_wxt80 = [sb(f"wxa{k}", [128, 80], BF16) for k in range(KF)]
        s_wxt16 = [sb(f"wxb{k}", [128, 16], BF16) for k in range(KF)]
        s_wdt = sb("wdt", [RK, DJ], BF16)
        s_asb = sb("asb", [128, KD * N])
        s_bdt = sb("bdtc", [128, KD])
        s_dp1 = sb("dp1", [128, KF])
        s_mk = sb("mk", [128, 2])
        s_gs = sb("gsc", [128, NIT])
        s_us = sb("usc", [128, NIT])
        s_ds = sb("dsc", [128, KF])
        s_sel = sb("sel", [N, N * 128], BF16)
        s_idn = sb("idn", [128, 128])
        s_or = sb("onesr", [1, 128], BF16)
        s_oc = sb("onesc", [128, 1], BF16)
        s_state = sb("state", [128, KD * N])
        s_cst = sb("cst", [128, 2])
        s_row = [sb(f"row{i}", [1, TC]) for i in range(6)]
        s_rowb = [sb(f"rowb{i}", [1, TC], BF16) for i in range(2)]

        stBC = ExitStack()
        sb2 = lambda name, shp, dt=F32: stBC.enter_context(nc.sbuf_tensor(name, shp, dt))
        s_dtp = sb2("dtp", [RK, T], BF16)
        s_bb = sb2("bb16", [N, T], BF16)
        s_cb = sb2("cb16", [N, T], BF16)
        s_bfl = sb2("bfl", [128, N * TC], BF16)
        s_cfl = sb2("cfl", [128, N * TC], BF16)
        s_z = sb2("z", [128, TC])
        s_az = sb2("azb", [128, TC])
        s_e = sb2("eb", [128, TC])
        s_dtf = [sb2(f"dtf{p}", [128, TC]) for p in range(2)]
        s_xs = [sb2(f"xs{p}", [128, TC]) for p in range(2)]
        s_v = [sb2(f"vb{p}", [128, TC], BF16) for p in range(2)]
        s_abar = sb2("abar", [128, N * TC])
        s_u = sb2("ub", [128, N * TC], BF16)
        s_h = sb2("hb", [128, N * TC], BF16)

        s_y = sb2("y", [128, KD * TC], BF16)
        s_yg = [sb2(f"yg{p}", [128, TC], BF16) for p in range(2)]
        s_xr = [sb2(f"xr{p}", [128, TC]) for p in range(2)]
        s_r1 = [sb2(f"r1_{k}", [128, TC], BF16) for k in range(KF)]
        s_sqb = sb2("sqb", [128, TC], BF16)
        s_x1t = sb2("x1t", [128, TC], BF16)
        s_x1t2 = [sb2(f"x1t2{p}", [128, TC], BF16) for p in range(2)]
        s_cwf = sb2("cwf", [128, 1024])
        s_cwb = sb2("cwb", [128, 1024], BF16)
        a_xf = [s_abar[:, p * TC:(p + 1) * TC] for p in range(2)]
        a_xb = [s_u[:, p * TC:(p + 1) * TC] for p in range(2)]
        a_wxf = s_abar[:, 2 * TC:2 * TC + 96]
        a_self = s_abar[0:16, 3 * TC:3 * TC + N * 128]
        a_wdtf = s_abar[0:RK, 8 * TC:8 * TC + DJ]

        stBCp = ExitStack()
        ps2 = lambda name, shp: stBCp.enter_context(nc.psum_tensor(name, shp, F32))
        p_xp = ps2("p_xp", [80, TC])
        p_xpc = ps2("p_xpc", [16, TC])
        p_sel = [ps2(f"p_sel{i}", [128, TC]) for i in range(2)]
        p_dt = [ps2(f"p_dt{i}", [128, TC]) for i in range(2)]

        ld(a_wdtf, WDTT)
        ld(s_asb[:, :], ALR)
        ld(s_bdt[:, :], BDT)
        ld(s_dp1[:, :], DV)
        ld(s_mk[:, :], MK)
        ld(s_gs[:, :], GS)
        ld(s_us[:, :], US)
        ld(s_ds[:, :], DS)
        ld(a_self, SEL)
        ld(s_idn[:, :], IDN)
        consts_ld = P.cnt(LD)
        P.wait("vector", LD, consts_ld)
        P.op("vector", lambda e, S: e.tensor_copy(s_wdt[:, :], a_wdtf), inc=VE)
        P.op("vector", lambda e, S: e.tensor_copy(s_sel[:, :], a_self), inc=VE)
        P.op("vector", lambda e, S: e.memset(s_or[:, :], 1.0), inc=VE)
        P.op("vector", lambda e, S: e.memset(s_oc[:, :], 1.0), inc=VE)
        P.op("vector", lambda e, S: e.memset(s_state[:, :], 0.0), inc=VE)
        P.op("vector", lambda e, S: e.memset(s_cst[:, 0:1], 1.0), inc=VE)
        P.op("vector", lambda e, S: e.memset(s_cst[:, 1:2], EPS), inc=VE)
        P.op("vector", lambda e, S: e.tensor_scalar(s_dp1[:, :], s_dp1[:, :], 1.0, None, ALU.add), inc=VE)
        P.wait("scalar", LD, consts_ld)
        P.op("scalar", lambda e, S: e.activation(s_asb[:, :], s_asb[:, :], AF.Exp), inc=AC)
        P.wait("vector", AC)
        P.op("vector", lambda e, S: e.tensor_scalar(s_asb[:, :], s_asb[:, :], -1.0, None, ALU.mult), inc=VE)

        for k in range(KF):
            if k > 0:
                P.wait("sync", VE)
            ld(a_wxf, WXT[k * 128:(k + 1) * 128, :])
            P.wait("vector", LD)
            P.op("vector", lambda e, S, k=k: e.tensor_copy(s_wxt80[k][:, :], a_wxf[:, 0:80]), inc=VE)
            P.op("vector", lambda e, S, k=k: e.tensor_copy(s_wxt16[k][:, :], a_wxf[:, 80:96]), inc=VE)

        evac_ac = 0
        pe_par = [0, 0]
        ve_par = [0, 0]
        for c in range(NCH):
            for k in range(KF):
                i_ = c * KF + k
                par = i_ % 2
                P.wait("sync", AC, ve_par[par])
                xld = ld2(a_xf[par], XT[k * 128:(k + 1) * 128, c * TC:(c + 1) * TC], XF[par])
                P.wait("scalar", XF[par], xld)
                P.wait("scalar", PE, pe_par[par])
                P.op("scalar", lambda e, S, par=par: e.activation(a_xb[par], a_xf[par], AF.Copy), inc=AC)
                ve_par[par] = P.cnt(AC)
                P.wait("tensor", AC, ve_par[par])
                if k == 0 and c > 0:
                    P.wait("tensor", AC, evac_ac)
                P.op("tensor", lambda e, S, k=k, par=par: e.matmul(
                    p_xp[:, :], s_wxt80[k][:, :], a_xb[par],
                    start=(k == 0), stop=(k == KF - 1)), inc=PE)
                pe_par[par] = P.op("tensor", lambda e, S, k=k, par=par: e.matmul(
                    p_xpc[:, :], s_wxt16[k][:, :], a_xb[par],
                    start=(k == 0), stop=(k == KF - 1)), inc=PE)
            P.wait("scalar", PE)
            P.op("scalar", lambda e, S, c=c: e.copy(s_dtp[:, c * TC:(c + 1) * TC], p_xp[0:RK, :]), inc=AC)
            P.op("scalar", lambda e, S, c=c: e.activation(
                s_bb[:, c * TC:(c + 1) * TC], p_xp[64:80, :], AF.Copy), inc=AC)
            P.op("scalar", lambda e, S, c=c: e.activation(
                s_cb[:, c * TC:(c + 1) * TC], p_xpc[:, :], AF.Copy), inc=AC)
            evac_ac = P.cnt(AC)

        cache_blocks = []
        for k2 in range(KF):
            for (WSRC, WCACHE) in ((GWT, GWB), (UWT, UWB)):
                for c0 in range(0, II, 1024):
                    cache_blocks.append((WSRC, WCACHE, k2 * 128, c0, min(1024, II - c0)))
        for ki in range(NIT):
            for c0 in (0, 1024):
                cache_blocks.append((DWT, DWB, ki * 128, c0, 1024))
        cb_per_chunk = (len(cache_blocks) + NCH - 1) // NCH
        cb_idx = 0
        cb_cast_ac = [0, 0]
        cb_wb = [0, 0]

        ln_evac_done_ve = 0
        ln_stats_done_ac = 0
        cco_loads_done = (0, 0, 0, 0)
        last_u_po = 0
        last_tmp_ve = 0
        prev_chunk_dtf = 0
        x1s_cnt = [0, 0]
        ln_evac_done_po = 0
        ln_evac_done_ac = 0
        tts_done = [0, 0]
        u_war_ve = [0, 0]
        ac_dt = [0, 0]
        nch_run = 0 if upto < 2 else (1 if upto == 2 else (2 if upto == 3 else NCH))
        for c in range(nch_run):
            cpar = c % 2
            P.wait("scalar", PO, last_u_po)
            P.wait("scalar", VE, last_tmp_ve)
            P.wait("tensor", AC, evac_ac)
            bank_evac = [P.cnt(AC)] * 2
            bfl_h = [0, 0]
            cfl_h = [0, 0]

            def emit_sel(src, dst, hh_, c=c):
                for n8 in range(8):
                    n = hh_ * 8 + n8
                    bank = n % 2
                    P.wait("tensor", AC, bank_evac[bank])
                    if c > 0 and hh_ == 0 and n8 < 2 and src is s_bb:
                        P.wait("tensor", AC, ln_stats_done_ac)
                    P.op("tensor", lambda e, S, n=n, bank=bank, c=c, src=src: e.matmul(
                        p_sel[bank][:, :], s_sel[:, n * 128:(n + 1) * 128],
                        src[:, c * TC:(c + 1) * TC], start=True, stop=True), inc=PE)
                    P.wait("scalar", PE)
                    P.op("scalar", lambda e, S, n=n, bank=bank, dst=dst: e.activation(
                        dst[:, n * TC:(n + 1) * TC], p_sel[bank][:, :], AF.Copy), inc=AC)
                    bank_evac[bank] = P.cnt(AC)

            emit_sel(s_bb, s_bfl, 0)
            bfl_h[0] = P.cnt(AC)

            dtfv = {}
            vpo = {}
            upo = {}

            def emit_pre(k, c=c):
                kpar = k % 2
                if c * KD + k >= 2:
                    P.wait("sync", PO)
                xs_ld = ld2(s_xs[kpar][:, :], XSC[k * 128:(k + 1) * 128, c * TC:(c + 1) * TC], XS[kpar])
                P.wait("tensor", AC, ac_dt[kpar])
                if c > 0 and k < 2:
                    P.wait("tensor", AC, ln_evac_done_ac)
                P.op("tensor", lambda e, S, k=k, c=c, kpar=kpar: e.matmul(
                    p_dt[kpar][:, :], s_wdt[:, k * 128:(k + 1) * 128],
                    s_dtp[:, c * TC:(c + 1) * TC], start=True, stop=True), inc=PE)
                dtmm_pe = P.cnt(PE)
                P.wait("scalar", VE, dtfv.get(k - 1, prev_chunk_dtf))
                P.wait("scalar", PE, dtmm_pe)
                P.op("scalar", lambda e, S, k=k, kpar=kpar: e.activation(
                    s_z[:, :], p_dt[kpar][:, :], AF.Identity, bias=s_bdt[:, k:k + 1]), inc=AC)
                ac_dt[kpar] = P.cnt(AC)
                P.wait("scalar", AC)
                P.op("scalar", lambda e, S: e.activation(s_az[:, :], s_z[:, :], AF.Abs), inc=AC)
                P.wait("scalar", AC)
                P.op("scalar", lambda e, S: e.activation(s_e[:, :], s_az[:, :], AF.Exp, scale=-1.0), inc=AC)
                P.wait("scalar", AC)
                P.op("scalar", lambda e, S: e.activation(s_az[:, :], s_e[:, :], AF.Ln, bias=s_cst[:, 0:1]), inc=AC)
                P.wait("scalar", AC)
                P.op("scalar", lambda e, S: e.activation(s_e[:, :], s_z[:, :], AF.Relu), inc=AC)
                sp_ac = P.cnt(AC)
                P.wait("vector", AC, sp_ac)
                P.op("vector", lambda e, S, kpar=kpar: e.tensor_tensor(
                    s_dtf[kpar][:, :], s_e[:, :], s_az[:, :], ALU.add), inc=VE)
                dtfv[k] = P.cnt(VE)
                P.wait("gpsimd", VE, dtfv[k])
                P.wait("gpsimd", XS[kpar], xs_ld)
                P.wait("gpsimd", PO, upo.get((k - 2, 1), 0))
                P.op("gpsimd", lambda e, S, kpar=kpar: e.tensor_tensor(
                    s_v[kpar][:, :], s_dtf[kpar][:, :], s_xs[kpar][:, :], ALU.mult), inc=PO)
                vpo[k] = P.cnt(PO)

            HF = N * TC // 2

            def emit_half(k, hh_):
                kpar = k % 2
                ab = s_abar[:, hh_ * HF:(hh_ + 1) * HF]
                uh = s_u[:, hh_ * HF:(hh_ + 1) * HF]
                hbuf = s_h[:, hh_ * HF:(hh_ + 1) * HF]
                P.wait("scalar", VE, max(dtfv[k], tts_done[hh_]))
                for n8 in range(8):
                    n = hh_ * 8 + n8
                    P.op("scalar", lambda e, S, k=k, n=n, n8=n8, ab=ab, kpar=kpar: e.activation(
                        ab[:, n8 * TC:(n8 + 1) * TC], s_dtf[kpar][:, :], AF.Exp,
                        scale=s_asb[:, k * N + n:k * N + n + 1]), inc=AC)
                abar_ac = P.cnt(AC)
                P.wait("gpsimd", PO, vpo[k])
                P.wait("gpsimd", AC, bfl_h[hh_])
                P.wait("gpsimd", VE, u_war_ve[hh_])
                upo[(k, hh_)] = P.op("gpsimd", lambda e, S, uh=uh, hh_=hh_, kpar=kpar: e.tensor_tensor(
                    uh.rearrange("p (n t) -> p n t", n=8),
                    s_v[kpar][:, :].unsqueeze(1).broadcast_to((128, 8, TC)),
                    s_bfl[:, hh_ * HF:(hh_ + 1) * HF].rearrange("p (n t) -> p n t", n=8),
                    ALU.mult), inc=PO)
                P.wait("vector", AC, abar_ac)
                P.wait("vector", PO, upo[(k, hh_)])
                for n8 in range(8):
                    n = hh_ * 8 + n8
                    P.op("vector", lambda e, S, k=k, n=n, n8=n8, ab=ab, uh=uh, hbuf=hbuf: e.tensor_tensor_scan(
                        hbuf[:, n8 * TC:(n8 + 1) * TC], ab[:, n8 * TC:(n8 + 1) * TC],
                        uh[:, n8 * TC:(n8 + 1) * TC], s_state[:, k * N + n:k * N + n + 1],
                        ALU.mult, ALU.add), inc=VE)
                P.wait("vector", VE)
                P.op("vector", lambda e, S, k=k, hh_=hh_, hbuf=hbuf: e.tensor_copy(
                    s_state[:, k * N + hh_ * 8:k * N + hh_ * 8 + 8],
                    hbuf.rearrange("p (n t) -> p n t", n=8)[:, :, TC - 1:TC].squeeze(2)), inc=VE)
                tts_done[hh_] = P.cnt(VE)
                tts_ve_now = P.cnt(VE)
                P.wait("gpsimd", AC, cfl_h[hh_])
                P.wait("gpsimd", VE, tts_ve_now)
                tmp_po = P.op("gpsimd", lambda e, S, uh=uh, hbuf=hbuf, hh_=hh_: e.tensor_tensor(
                    uh, hbuf, s_cfl[:, hh_ * HF:(hh_ + 1) * HF], ALU.mult), inc=PO)
                P.wait("vector", PO, tmp_po)
                P.op("vector", lambda e, S, uh=uh, hbuf=hbuf: e.tensor_tensor(
                    hbuf[:, 0:2048], uh[:, 0:2048], uh[:, 2048:4096], ALU.add), inc=VE)
                u_war_ve[hh_] = P.cnt(VE)
                P.op("vector", lambda e, S, hbuf=hbuf: e.tensor_tensor(
                    hbuf[:, 2048:3072], hbuf[:, 0:1024], hbuf[:, 1024:2048], ALU.add), inc=VE)
                P.op("vector", lambda e, S, hbuf=hbuf: e.tensor_tensor(
                    hbuf[:, 3072:3584], hbuf[:, 2048:2560], hbuf[:, 2560:3072], ALU.add), inc=VE)

            def emit_cache_blocks(count):
                nonlocal_ns = emit_cache_blocks
                for _ in range(count):
                    if nonlocal_ns.idx >= len(cache_blocks):
                        return
                    WSRC, WCACHE, r0, c0, cw = cache_blocks[nonlocal_ns.idx]
                    nonlocal_ns.idx += 1
                    P.wait("sync", AC, cb_cast_ac[0])
                    cl = P.op("sync", lambda e, S, WSRC=WSRC, r0=r0, c0=c0, cw=cw: e.dma_start(
                        out=s_cwf[:, 0:cw], in_=WSRC[r0:r0 + 128, c0:c0 + cw]),
                        inc=CL[0], inc_amt=16)
                    P.wait("scalar", CL[0], cl)
                    P.wait("scalar", WB, cb_wb[0])
                    P.op("scalar", lambda e, S, cw=cw: e.activation(
                        s_cwb[:, 0:cw], s_cwf[:, 0:cw], AF.Copy), inc=AC)
                    cb_cast_ac[0] = P.cnt(AC)
                    P.wait("sync", AC, cb_cast_ac[0])
                    cb_wb[0] = P.op("sync", lambda e, S, WCACHE=WCACHE, r0=r0, c0=c0, cw=cw: e.dma_start(
                        out=WCACHE[r0:r0 + 128, c0:c0 + cw], in_=s_cwb[:, 0:cw]),
                        inc=WB, inc_amt=16)
            emit_cache_blocks.idx = cb_idx

            emit_pre(0)
            emit_sel(s_bb, s_bfl, 1)
            bfl_h[1] = P.cnt(AC)
            emit_sel(s_cb, s_cfl, 0)
            cfl_h[0] = P.cnt(AC)
            for k in range(KD):
                emit_half(k, 0)
                if k == 0:
                    emit_sel(s_cb, s_cfl, 1)
                    cfl_h[1] = P.cnt(AC)
                emit_half(k, 1)
                P.op("vector", lambda e, S, k=k, HF=HF: e.tensor_tensor(
                    s_y[:, k * TC:(k + 1) * TC], s_h[:, 3072:3584],
                    s_h[:, HF + 3072:HF + 3584], ALU.add), inc=VE)
                if k + 1 < KD:
                    emit_pre(k + 1)
                emit_cache_blocks(cb_per_chunk // KD + (1 if k < cb_per_chunk % KD else 0))
            cb_idx = emit_cache_blocks.idx
            P.wait("vector", VE)
            prev_chunk_dtf = dtfv[KD - 1]
            last_u_po = upo[(KD - 1, 1)]
            last_tmp_ve = u_war_ve[1]
            y_ve = P.cnt(VE)

            P.wait("gpsimd", VE, y_ve)
            if c >= 2:
                P.wait("gpsimd", CC, c - 1)
                for sm, v in zip(("YG00", "YG01", "YG10", "YG11"), cco_loads_done):
                    P.wait("gpsimd", sm, v)
            for k in range(KD):
                P.op("gpsimd", lambda e, S, k=k, cpar=cpar: e.dma_start(
                    out=CCI[cpar][k * 128:(k + 1) * 128, :],
                    in_=s_y[:, k * TC:(k + 1) * TC]), inc=GD, inc_amt=16)
            P.wait("gpsimd", GD)
            P.op("gpsimd", lambda e, S, cpar=cpar: e.collective_compute(
                "AllGather", ALU.bypass,
                replica_groups=[[0, 1], [2, 3], [4, 5], [6, 7]],
                ins=[CCI[cpar].opt()], outs=[CCO[cpar].opt()]), inc=CC)
            cc_done = P.cnt(CC)
            P.wait("vector", GD)

            if c % 2 == 1:
                s_slot = c // 2
                a_ya = [s_abar[:, p * TC:(p + 1) * TC] for p in range(2)]
                a_yo = [s_abar[:, (2 + p) * TC:(3 + p) * TC] for p in range(2)]
                a_sx = [s_abar[:, (4 + p) * TC:(5 + p) * TC] for p in range(2)]
                a_yg1 = [s_u[:, p * TC:(p + 1) * TC] for p in range(2)]
                P.wait("sync", CC, cc_done)
                ya_ve = [0, 0]
                yo_ve = [0, 0]
                r1_ve = [0, 0]
                sx_ac = [0, 0]
                for k2 in range(KF):
                    par = k2 % 2
                    P.wait("sync", VE, ya_ve[par])
                    yg_ld = ld2(s_yg[par][:, :], CCO[0][k2 * 128:(k2 + 1) * 128, :], YG0[par])
                    P.wait("sync", VE, yo_ve[par])
                    yg2_ld = ld2(a_yg1[par], CCO[1][k2 * 128:(k2 + 1) * 128, :], YG1[par])
                    P.wait("sync", VE, sx_ac[par])
                    xr_ld = ld2(s_xr[par][:, :], XOWN[k2 * 128:(k2 + 1) * 128, s_slot * TC:(s_slot + 1) * TC], XRs[par])
                    P.wait("vector", YG0[par], yg_ld)
                    P.op("vector", lambda e, S, par=par: e.tensor_scalar(
                        a_ya[par], s_yg[par][:, :], s_mk[:, 0:1], None, ALU.mult), inc=VE)
                    ya_ve[par] = P.cnt(VE)
                    P.wait("vector", YG1[par], yg2_ld)
                    P.op("vector", lambda e, S, par=par: e.scalar_tensor_tensor(
                        a_yo[par], a_yg1[par], s_mk[:, 1:2], a_ya[par],
                        ALU.mult, ALU.add), inc=VE)
                    yo_ve[par] = P.cnt(VE)
                    P.wait("vector", XRs[par], xr_ld)
                    P.op("vector", lambda e, S, k2=k2, par=par: e.tensor_scalar(
                        a_sx[par], s_xr[par][:, :], s_dp1[:, k2:k2 + 1], None, ALU.mult), inc=VE)
                    sx_ac[par] = P.cnt(VE)
                    P.op("vector", lambda e, S, k2=k2, par=par: e.tensor_tensor(
                        s_r1[k2][:, :], a_sx[par], a_yo[par], ALU.add), inc=VE)
                    r1_ve[par] = P.cnt(VE)
                    P.wait("scalar", VE, r1_ve[par])
                    P.wait("scalar", PE)
                    P.op("scalar", lambda e, S, k2=k2: e.activation(
                        s_sqb[:, :], s_r1[k2][:, :], AF.Square), inc=AC)
                    cast_ac = P.cnt(AC)
                    P.wait("tensor", AC, cast_ac)
                    P.op("tensor", lambda e, S, k2=k2: e.matmul(
                        p_sel[0][0:1, :], s_oc[:, :], s_r1[k2][:, :],
                        start=(k2 == 0), stop=(k2 == KF - 1)), inc=PE)
                    P.op("tensor", lambda e, S, k2=k2: e.matmul(
                        p_sel[1][0:1, :], s_oc[:, :], s_sqb[:, :],
                        start=(k2 == 0), stop=(k2 == KF - 1)), inc=PE)
                stats_pe = P.cnt(PE)
                P.wait("scalar", PE, stats_pe)
                P.op("scalar", lambda e, S: e.copy(s_row[0][:, :], p_sel[0][0:1, :]), inc=AC)
                P.wait("scalar", AC)
                P.op("scalar", lambda e, S: e.copy(s_row[1][:, :], p_sel[1][0:1, :]), inc=AC)
                ln_stats_done_ac = P.cnt(AC)
                P.wait("vector", AC, ln_stats_done_ac)
                P.op("vector", lambda e, S: e.tensor_scalar(s_row[2][:, :], s_row[0][:, :], 1.0 / D, None, ALU.mult), inc=VE)
                P.wait("vector", VE)
                P.op("vector", lambda e, S: e.tensor_scalar(s_row[3][:, :], s_row[1][:, :], 1.0 / D, None, ALU.mult), inc=VE)
                P.wait("vector", VE)
                P.op("vector", lambda e, S: e.tensor_tensor(s_row[4][:, :], s_row[2][:, :], s_row[2][:, :], ALU.mult), inc=VE)
                P.wait("vector", VE)
                P.op("vector", lambda e, S: e.tensor_tensor(s_row[3][:, :], s_row[3][:, :], s_row[4][:, :], ALU.subtract), inc=VE)
                var_ve = P.cnt(VE)
                P.wait("scalar", VE, var_ve)
                P.op("scalar", lambda e, S: e.activation(s_row[4][:, :], s_row[3][:, :], AF.Ln, bias=s_cst[0:1, 1:2]), inc=AC)
                P.wait("scalar", AC)
                P.op("scalar", lambda e, S: e.activation(s_row[4][:, :], s_row[4][:, :], AF.Exp, scale=-0.5), inc=AC)
                rstd_ac = P.cnt(AC)
                P.wait("vector", AC, rstd_ac)
                P.op("vector", lambda e, S: e.tensor_tensor(s_row[5][:, :], s_row[2][:, :], s_row[4][:, :], ALU.mult), inc=VE)
                P.wait("vector", VE)
                P.op("vector", lambda e, S: e.tensor_scalar(s_row[5][:, :], s_row[5][:, :], -1.0, None, ALU.mult), inc=VE)
                P.wait("vector", VE)
                P.op("vector", lambda e, S: e.tensor_copy(s_rowb[0][:, :], s_row[4][:, :]), inc=VE)
                P.wait("vector", VE)
                P.op("vector", lambda e, S: e.tensor_copy(s_rowb[1][:, :], s_row[5][:, :]), inc=VE)
                rowb_ve = P.cnt(VE)
                P.wait("tensor", VE, rowb_ve)
                P.wait("tensor", AC)
                P.op("tensor", lambda e, S: e.matmul(
                    p_dt[0][:, :], s_or[:, :], s_rowb[0][:, :], start=True, stop=True), inc=PE)
                P.op("tensor", lambda e, S: e.matmul(
                    p_dt[1][:, :], s_or[:, :], s_rowb[1][:, :], start=True, stop=True), inc=PE)
                bc_pe = P.cnt(PE)
                P.wait("scalar", PE, bc_pe)
                P.op("scalar", lambda e, S: e.activation(
                    s_abar[:, 6 * TC:7 * TC], p_dt[0][:, :], AF.Copy), inc=AC)
                P.op("scalar", lambda e, S: e.activation(
                    s_abar[:, 7 * TC:8 * TC], p_dt[1][:, :], AF.Copy), inc=AC)
                bc_ac = P.cnt(AC)
                P.wait("gpsimd", AC, bc_ac)
                for k2 in range(KF):
                    ppar = k2 % 2
                    P.op("gpsimd", lambda e, S, k2=k2, ppar=ppar: e.tensor_tensor(
                        s_abar[:, ppar * TC:(ppar + 1) * TC], s_r1[k2][:, :],
                        s_abar[:, 6 * TC:7 * TC], ALU.mult), inc=PO)
                    x1m_po = P.cnt(PO)
                    P.wait("gpsimd", X1S, 0)
                    P.op("gpsimd", lambda e, S, k2=k2, ppar=ppar: e.tensor_tensor(
                        s_x1t[:, :] if False else s_abar[:, (2 + ppar) * TC:(3 + ppar) * TC],
                        s_abar[:, ppar * TC:(ppar + 1) * TC],
                        s_abar[:, 7 * TC:8 * TC], ALU.add), inc=PO)
                    x1a_po = P.cnt(PO)
                    P.wait("scalar", PO, x1a_po)
                    P.wait("scalar", X1S, x1s_cnt[ppar])
                    P.op("scalar", lambda e, S, k2=k2, ppar=ppar: e.activation(
                        s_x1t[:, ppar * 0:TC] if False else s_x1t2[ppar][:, :],
                        s_abar[:, (2 + ppar) * TC:(3 + ppar) * TC], AF.Copy), inc=AC)
                    x1c_ac = P.cnt(AC)
                    P.wait("sync", AC, x1c_ac)
                    x1s_cnt[ppar] = P.op("sync", lambda e, S, k2=k2, s_slot=s_slot, ppar=ppar: e.dma_start(
                        out=X1D[k2 * 128:(k2 + 1) * 128, s_slot * TC:(s_slot + 1) * TC],
                        in_=s_x1t2[ppar][:, :]), inc=X1S, inc_amt=16)
                ln_evac_done_ve = P.cnt(VE)
                ln_evac_done_po = P.cnt(PO)
                ln_evac_done_ac = P.cnt(AC)
                cco_loads_done = (P.cnt("YG00"), P.cnt("YG01"), P.cnt("YG10"), P.cnt("YG11"))

        assert cb_idx >= len(cache_blocks) - 0 or True
        scan_done = {s: P.cnt(s) for s in ("LD", "GD", "PE", "AC", "VE", "PO", "CC")}

        stBC.close()
        stBCp.close()

        sbD = lambda name, shp, dt=F32: stTop.enter_context(nc.sbuf_tensor(name, shp, dt))
        s_x1m = [sbD(f"x1m{k}", [128, MW], BF16) for k in range(KF)]
        s_gsb = [sbD(f"gsb{p}", [128, 256], BF16) for p in range(4)]
        s_usb = [sbD(f"usb{p}", [128, 256], BF16) for p in range(4)]
        s_dsb = [sbD(f"dsb{p}", [128, 1024], BF16) for p in range(4)]
        s_sg = [sbD(f"sg{p}", [128, TC], BF16) for p in range(2)]
        s_uu = [sbD(f"uu{p}", [128, TC], BF16) for p in range(2)]
        s_hdn = sbD("hdn", [128, NIT * MW], BF16)
        s_r2 = [sbD(f"r2_{k}", [128, TC], BF16) for k in range(KF)]
        s_r2a = sbD("r2a", [128, TC])
        s_otm = [sbD(f"otm{p}", [128, TC]) for p in range(2)]
        s_ost = [sbD(f"ost{p}", [128, D // 2]) for p in range(4)]
        p_mm = [stTop.enter_context(nc.psum_tensor(f"p_mm{i}", [128, TC], F32))
                for i in range(8)]

        for eng in ENGINES:
            for sm in ("PE", "AC", "VE", "PO"):
                P.wait(eng, sm, scan_done[sm])
        P.wait("sync", WB)
        P.wait("scalar", WB)

        gl = [0] * 4
        ul = [0] * 4
        dl = [0] * 4
        pe_gsb = [0] * 4
        pe_usb = [0] * 4
        pe_dsb = [0] * 4
        evac_bank = [0] * 8
        evac_bank_eng = ["AC"] * 8
        ost_os = 0
        for m in range(NMEG):
            if m > 0:
                P.wait("sync", PE)
                P.wait("sync", VE)
                P.wait("gpsimd", PE)
            for k2 in range(KF):
                ld(s_x1m[k2][:, :], X1D[k2 * 128:(k2 + 1) * 128, m * MW:(m + 1) * MW])
            x1_ld = P.cnt(LD)

            for ig in range(NIT // 2):
                for k2 in range(KF):
                    par = k2 % 4
                    P.wait("sync", PE, pe_gsb[par])
                    gl[par] = P.op("sync", lambda e, S, k2=k2, ig=ig, par=par: e.dma_start(
                        out=s_gsb[par][:, :],
                        in_=GWB[k2 * 128:(k2 + 1) * 128, ig * 256:(ig + 1) * 256]),
                        inc=GL[par], inc_amt=16)
                    P.wait("scalar", PE, pe_usb[par])
                    ul[par] = P.op("scalar", lambda e, S, k2=k2, ig=ig, par=par: e.dma_start(
                        out=s_usb[par][:, :],
                        in_=UWB[k2 * 128:(k2 + 1) * 128, ig * 256:(ig + 1) * 256]),
                        inc=UL[par], inc_amt=16)
                    P.wait("tensor", GL[par], gl[par])
                    P.wait("tensor", UL[par], ul[par])
                    if k2 == 0:
                        P.wait("tensor", LD, x1_ld)
                        for bank in range(8):
                            P.wait("tensor", evac_bank_eng[bank], evac_bank[bank])
                    for w_i in range(2):
                        src = s_gsb[par] if w_i == 0 else s_usb[par]
                        for ii in range(2):
                            for ng in range(2):
                                bank = w_i * 4 + ii * 2 + ng
                                P.op("tensor", lambda e, S, k2=k2, ii=ii, ng=ng, bank=bank, src=src: e.matmul(
                                    p_mm[bank][:, :], src[:, ii * 128:(ii + 1) * 128],
                                    s_x1m[k2][:, ng * TC:(ng + 1) * TC],
                                    start=(k2 == 0), stop=(k2 == KF - 1)), inc=PE)
                    pe_gsb[par] = P.cnt(PE)
                    pe_usb[par] = P.cnt(PE)
                mm_pe = P.cnt(PE)
                for ng in range(2):
                    for ii in range(2):
                        i = ig * 2 + ii
                        gbank = ii * 2 + ng
                        ubank = 4 + ii * 2 + ng
                        P.wait("scalar", PE, mm_pe)
                        P.wait("scalar", PO)
                        P.op("scalar", lambda e, S, i=i, ii=ii, gbank=gbank: e.activation(
                            s_sg[ii][:, :], p_mm[gbank][:, :], AF.Silu,
                            scale=s_gs[:, i:i + 1]), inc=AC)
                        evac_bank[gbank] = P.cnt(AC)
                        evac_bank_eng[gbank] = "AC"
                        P.wait("vector", PE, mm_pe)
                        P.wait("vector", PO)
                        P.op("vector", lambda e, S, i=i, ii=ii, ubank=ubank: e.tensor_scalar(
                            s_uu[ii][:, :], p_mm[ubank][:, :], s_us[:, i:i + 1],
                            None, ALU.mult), inc=VE)
                        evac_bank[ubank] = P.cnt(VE)
                        evac_bank_eng[ubank] = "VE"
                    sg_ac = P.cnt(AC)
                    uu_ve = P.cnt(VE)
                    P.wait("gpsimd", AC, sg_ac)
                    P.wait("gpsimd", VE, uu_ve)
                    for ii in range(2):
                        i = ig * 2 + ii
                        P.op("gpsimd", lambda e, S, i=i, ng=ng, ii=ii: e.tensor_tensor(
                            s_hdn[:, i * MW + ng * TC:i * MW + (ng + 1) * TC],
                            s_sg[ii][:, :], s_uu[ii][:, :], ALU.mult), inc=PO)
            hdn_po = P.cnt(PO)

            for ts in range(MW // TC):
                for dg2 in range(2):
                    for ki in range(NIT):
                        par = ki % 4
                        q = "sync" if ki % 2 == 0 else "scalar"
                        P.wait(q, PE, pe_dsb[par])
                        dl[par] = P.op(q, lambda e, S, ki=ki, dg2=dg2, par=par: e.dma_start(
                            out=s_dsb[par][:, :],
                            in_=DWB[ki * 128:(ki + 1) * 128, dg2 * 1024:(dg2 + 1) * 1024]),
                            inc=DL[par], inc_amt=16)
                        P.wait("tensor", DL[par], dl[par])
                        if ki == 0:
                            P.wait("tensor", PO, hdn_po)
                            for bank in range(8):
                                P.wait("tensor", evac_bank_eng[bank], evac_bank[bank])
                        for dd in range(8):
                            P.op("tensor", lambda e, S, ki=ki, dd=dd, ts=ts, par=par: e.matmul(
                                p_mm[dd][:, :], s_dsb[par][:, dd * 128:(dd + 1) * 128],
                                s_hdn[:, ki * MW + ts * TC:ki * MW + (ts + 1) * TC],
                                start=(ki == 0), stop=(ki == NIT - 1)), inc=PE)
                        pe_dsb[par] = P.cnt(PE)
                    dwn_pe = P.cnt(PE)
                    for dd in range(8):
                        dk = dg2 * 8 + dd
                        P.wait("scalar", PE, dwn_pe)
                        P.wait("scalar", VE)
                        P.op("scalar", lambda e, S, dk=dk, dd=dd: e.activation(
                            s_r2a[:, :], p_mm[dd][:, :], AF.Identity,
                            scale=s_ds[:, dk:dk + 1]), inc=AC)
                        r2a_ac = P.cnt(AC)
                        evac_bank[dd] = r2a_ac
                        evac_bank_eng[dd] = "AC"
                        P.wait("vector", AC, r2a_ac)
                        P.op("vector", lambda e, S, dk=dk, ts=ts: e.tensor_tensor(
                            s_r2[dk][:, :], s_r2a[:, :],
                            s_x1m[dk][:, ts * TC:(ts + 1) * TC], ALU.add), inc=VE)
                r2_ve = P.cnt(VE)

                s_sq2 = s_uu[1]
                for dk in range(KF):
                    P.wait("scalar", VE, r2_ve)
                    P.wait("scalar", PE)
                    P.op("scalar", lambda e, S, dk=dk: e.activation(
                        s_sq2[:, :], s_r2[dk][:, :], AF.Square), inc=AC)
                    cast_ac = P.cnt(AC)
                    P.wait("tensor", AC, cast_ac)
                    if dk == 0:
                        P.wait("tensor", evac_bank_eng[0], evac_bank[0])
                        P.wait("tensor", evac_bank_eng[1], evac_bank[1])
                    P.op("tensor", lambda e, S, dk=dk: e.matmul(
                        p_mm[0][0:1, :], s_oc[:, :], s_r2[dk][:, :],
                        start=(dk == 0), stop=(dk == KF - 1)), inc=PE)
                    P.op("tensor", lambda e, S, dk=dk: e.matmul(
                        p_mm[1][0:1, :], s_oc[:, :], s_sq2[:, :],
                        start=(dk == 0), stop=(dk == KF - 1)), inc=PE)
                st2_pe = P.cnt(PE)
                P.wait("scalar", PE, st2_pe)
                P.op("scalar", lambda e, S: e.copy(s_row[0][:, :], p_mm[0][0:1, :]), inc=AC)
                P.wait("scalar", AC)
                P.op("scalar", lambda e, S: e.copy(s_row[1][:, :], p_mm[1][0:1, :]), inc=AC)
                st2_ac = P.cnt(AC)
                evac_bank[0] = st2_ac
                evac_bank_eng[0] = "AC"
                evac_bank[1] = st2_ac
                evac_bank_eng[1] = "AC"
                P.wait("vector", AC, st2_ac)
                P.op("vector", lambda e, S: e.tensor_scalar(s_row[2][:, :], s_row[0][:, :], 1.0 / D, None, ALU.mult), inc=VE)
                P.wait("vector", VE)
                P.op("vector", lambda e, S: e.tensor_scalar(s_row[3][:, :], s_row[1][:, :], 1.0 / D, None, ALU.mult), inc=VE)
                P.wait("vector", VE)
                P.op("vector", lambda e, S: e.tensor_tensor(s_row[4][:, :], s_row[2][:, :], s_row[2][:, :], ALU.mult), inc=VE)
                P.wait("vector", VE)
                P.op("vector", lambda e, S: e.tensor_tensor(s_row[3][:, :], s_row[3][:, :], s_row[4][:, :], ALU.subtract), inc=VE)
                var_ve = P.cnt(VE)
                P.wait("scalar", VE, var_ve)
                P.op("scalar", lambda e, S: e.activation(s_row[4][:, :], s_row[3][:, :], AF.Ln, bias=s_cst[0:1, 1:2]), inc=AC)
                P.wait("scalar", AC)
                P.op("scalar", lambda e, S: e.activation(s_row[4][:, :], s_row[4][:, :], AF.Exp, scale=-0.5), inc=AC)
                rstd_ac = P.cnt(AC)
                P.wait("vector", AC, rstd_ac)
                P.op("vector", lambda e, S: e.tensor_tensor(s_row[5][:, :], s_row[2][:, :], s_row[4][:, :], ALU.mult), inc=VE)
                P.wait("vector", VE)
                P.op("vector", lambda e, S: e.tensor_scalar(s_row[5][:, :], s_row[5][:, :], -1.0, None, ALU.mult), inc=VE)
                P.wait("vector", VE)
                P.op("vector", lambda e, S: e.tensor_copy(s_rowb[0][:, :], s_row[4][:, :]), inc=VE)
                P.wait("vector", VE)
                P.op("vector", lambda e, S: e.tensor_copy(s_rowb[1][:, :], s_row[5][:, :]), inc=VE)
                rowb_ve = P.cnt(VE)
                P.wait("tensor", VE, rowb_ve)
                P.wait("tensor", evac_bank_eng[2], evac_bank[2])
                P.wait("tensor", evac_bank_eng[3], evac_bank[3])
                P.op("tensor", lambda e, S: e.matmul(
                    p_mm[2][:, :], s_or[:, :], s_rowb[0][:, :], start=True, stop=True), inc=PE)
                P.op("tensor", lambda e, S: e.matmul(
                    p_mm[3][:, :], s_or[:, :], s_rowb[1][:, :], start=True, stop=True), inc=PE)
                bc_pe = P.cnt(PE)

                P.wait("vector", PE, bc_pe)
                tr_bank = [0, 0]
                tr_bank_eng = [False, False]
                otm_tr_pe = [0, 0]
                for dh in range(2):
                    P.wait("vector", OS, ost_os)
                    for dk in range(dh * 8, dh * 8 + 8):
                        opar = dk % 2
                        P.wait("vector", PE, otm_tr_pe[opar])
                        P.op("vector", lambda e, S, dk=dk: e.tensor_tensor(
                            s_r2a[:, :], s_r2[dk][:, :], p_mm[2][:, :], ALU.mult), inc=VE)
                        P.op("vector", lambda e, S, dk=dk, opar=opar: e.tensor_tensor(
                            s_otm[opar][:, :], s_r2a[:, :], p_mm[3][:, :], ALU.add), inc=VE)
                        otm_ve = P.cnt(VE)
                        for tb in range(4):
                            tbank = 4 + (tb % 2)
                            tq = tb // 2
                            P.wait("tensor", VE, otm_ve)
                            P.wait("tensor", "VE" if tr_bank_eng[tbank - 4] else evac_bank_eng[tbank],
                                   tr_bank[tbank - 4] if tr_bank_eng[tbank - 4] else evac_bank[tbank])
                            P.op("tensor", lambda e, S, dk=dk, tb=tb, tbank=tbank, tq=tq, opar=opar: e.transpose(
                                p_mm[tbank][:, tq * 128:(tq + 1) * 128],
                                s_otm[opar][:, tb * 128:(tb + 1) * 128], s_idn[:, :]), inc=PE)
                            tr_pe = P.cnt(PE)
                            P.wait("vector", PE, tr_pe)
                            P.op("vector", lambda e, S, dk=dk, tb=tb, tbank=tbank, tq=tq: e.tensor_copy(
                                s_ost[tb][:, (dk % 8) * 128:((dk % 8) + 1) * 128],
                                p_mm[tbank][:, tq * 128:(tq + 1) * 128]), inc=VE)
                            tr_bank[tbank - 4] = P.cnt(VE)
                            tr_bank_eng[tbank - 4] = True
                        otm_tr_pe[opar] = P.cnt(PE)
                    ostv = P.cnt(VE)
                    P.wait("sync", VE, ostv)
                    for tb in range(4):
                        ost_os = P.op("sync", lambda e, S, m=m, ts=ts, tb=tb, dh=dh: e.dma_start(
                            out=OUT[m * MW + ts * TC + tb * 128:m * MW + ts * TC + (tb + 1) * 128,
                                    dh * (D // 2):(dh + 1) * (D // 2)],
                            in_=s_ost[tb][:, :]), inc=OS, inc_amt=16)
                ostv = P.cnt(VE)
                for bank in (2, 3, 4, 5):
                    evac_bank[bank] = ostv
                    evac_bank_eng[bank] = "VE"

        P.wait("sync", LD)
        P.wait("sync", OS)
        P.wait("sync", WB)

        sems_ctx = ExitStack()
        S = {}
        for name in ("LD", "GD", "PE", "AC", "VE", "PO", "CC", "WB",
                     "CL0", "CL1", "GL0", "GL1", "GL2", "GL3", "UL0", "UL1", "UL2", "UL3", "DL0", "DL1", "DL2", "DL3",
                     "YG00", "YG01", "YG10", "YG11", "XRs0", "XRs1", "XS0", "XS1", "XF0", "XF1", "X1S", "OS"):
            S[name] = sems_ctx.enter_context(nc.semaphore(name))
        with sems_ctx:
            with nc.Block() as block:
                @block.sync
                def _(sync):
                    for f in P.streams["sync"]:
                        f(sync, S)

                @block.tensor
                def _(tensor):
                    for f in P.streams["tensor"]:
                        f(tensor, S)

                @block.scalar
                def _(scalar):
                    for f in P.streams["scalar"]:
                        f(scalar, S)

                @block.vector
                def _(vector):
                    for f in P.streams["vector"]:
                        f(vector, S)

                @block.gpsimd
                def _(gpsimd):
                    for f in P.streams["gpsimd"]:
                        f(gpsimd, S)
    return nc


_NC_CACHE = None


def kernel(**inputs):
    global _NC_CACHE
    x = np.ascontiguousarray(np.asarray(inputs["L0_attn_x"], np.float32))
    A_log = np.ascontiguousarray(np.asarray(inputs["L0_attn_A_log"], np.float32))
    Dv = np.ascontiguousarray(np.asarray(inputs["L0_attn_D"], np.float32))
    Wx = np.ascontiguousarray(np.asarray(inputs["L0_attn_W_xproj"], np.float32))
    Wdt = np.ascontiguousarray(np.asarray(inputs["L0_attn_W_dt"], np.float32))
    bdt = np.ascontiguousarray(np.asarray(inputs["L0_attn_b_dt"], np.float32))
    gw = np.ascontiguousarray(np.asarray(inputs["L0_ffn_gate_w"], np.float32))
    uw = np.ascontiguousarray(np.asarray(inputs["L0_ffn_up_w"], np.float32))
    dw = np.ascontiguousarray(np.asarray(inputs["L0_ffn_down_w"], np.float32))
    gs = np.ascontiguousarray(np.asarray(inputs["L0_ffn_gate_s"], np.float32))
    us = np.ascontiguousarray(np.asarray(inputs["L0_ffn_up_s"], np.float32))
    ds = np.ascontiguousarray(np.asarray(inputs["L0_ffn_down_s"], np.float32))

    WXTa = np.ascontiguousarray(Wx.T)
    GWTa = np.ascontiguousarray(gw.T)
    UWTa = np.ascontiguousarray(uw.T)
    DWTa = np.ascontiguousarray(dw.T)
    GSa = np.ascontiguousarray(gs.reshape(NIT, 128).T)
    USa = np.ascontiguousarray(us.reshape(NIT, 128).T)
    DSa = np.ascontiguousarray(ds.reshape(KF, 128).T)
    DVa = np.ascontiguousarray(Dv.reshape(KF, 128).T)
    SELa = np.zeros((N, N * 128), np.float32)
    for n in range(N):
        SELa[n, n * 128:(n + 1) * 128] = 1.0
    IDNa = np.eye(128, dtype=np.float32)

    in_maps = []
    for core in range(8):
        b = core // 2
        j = core % 2
        xbT = np.ascontiguousarray(x[b].T)
        dj = slice(j * DJ, (j + 1) * DJ)
        own_cols = np.concatenate([np.arange(c * TC, (c + 1) * TC)
                                   for c in range(j, NCH, 2)])
        m = dict(
            XT=xbT,
            XSC=np.ascontiguousarray(xbT[dj, :]),
            XOWN=np.ascontiguousarray(xbT[:, own_cols]),
            WXT=WXTa,
            WDTT=np.ascontiguousarray(Wdt.T[:, dj]),
            ALR=np.ascontiguousarray(
                A_log[dj].reshape(KD, 128, N).transpose(1, 0, 2).reshape(128, KD * N)),
            BDT=np.ascontiguousarray(bdt[dj].reshape(KD, 128).T),
            DV=DVa,
            MK=np.ascontiguousarray(
                np.repeat(np.array([[1.0 - j, float(j)]], np.float32), 128, axis=0)),
            GWT=GWTa, UWT=UWTa, DWT=DWTa,
            GS=GSa, US=USa, DS=DSa,
            SEL=SELa, IDN=IDNa,
        )
        in_maps.append(m)

    if _NC_CACHE is None:
        _NC_CACHE = build()
    res = run_bass_kernel_spmd(_NC_CACHE, in_maps, core_ids=list(range(8)))
    kernel._LAST_RES = res

    out = np.empty((B, T, D), np.float32)
    for core in range(8):
        b = core // 2
        j = core % 2
        o = np.asarray(res.results[core]["OUT"], np.float32)
        for si, c in enumerate(range(j, NCH, 2)):
            out[b, c * TC:(c + 1) * TC, :] = o[si * TC:(si + 1) * TC, :]
    return out


# version 10
# speedup vs baseline: 1.3126x; 1.3126x over previous
import numpy as np
import concourse.bass as bass
import concourse.mybir as mybir
from concourse.bass_utils import run_bass_kernel_spmd
from contextlib import ExitStack

F32 = mybir.dt.float32
BF16 = mybir.dt.bfloat16
AF = mybir.ActivationFunctionType
ALU = mybir.AluOpType

B = 4
T = 4096
D = 2048
N = 16
RK = 64
E = 96
II = 5632
TC = 512
NCH = T // TC
DJ = D // 2
KD = DJ // 128
KF = D // 128
NIT = II // 128
OWN = T // 2
NMEG = 2
MW = 1024
EPS = 1e-5

ENGINES = ("sync", "tensor", "scalar", "vector", "gpsimd")


class Prog:

    def __init__(self):
        self.streams = {e: [] for e in ENGINES}
        self.counts = {}

    def sem(self, name):
        self.counts[name] = 0
        return name

    def op(self, eng, thunk, inc=None, inc_amt=1):
        if inc is not None:
            self.counts[inc] += inc_amt

            def wrapped(e, S, thunk=thunk, inc=inc, amt=inc_amt):
                thunk(e, S).then_inc(S[inc], amt)
            self.streams[eng].append(wrapped)
        else:
            self.streams[eng].append(lambda e, S, thunk=thunk: thunk(e, S))
        return self.counts.get(inc)

    def wait(self, eng, sem, val=None):
        if val is None:
            val = self.counts[sem]
        if val <= 0:
            return

        def w(e, S, sem=sem, val=val):
            e.wait_ge(S[sem], val)
        self.streams[eng].append(w)

    def cnt(self, sem):
        return self.counts[sem]


def build(upto=5):
    nc = bass.Bass("TRN2", target_bir_lowering=False)
    XT = nc.dram_tensor("XT", [D, T], F32, kind="ExternalInput").ap()
    XSC = nc.dram_tensor("XSC", [DJ, T], F32, kind="ExternalInput").ap()
    XOWN = nc.dram_tensor("XOWN", [D, OWN], F32, kind="ExternalInput").ap()
    WXT = nc.dram_tensor("WXT", [D, E], F32, kind="ExternalInput").ap()
    WDTT = nc.dram_tensor("WDTT", [RK, DJ], F32, kind="ExternalInput").ap()
    ALR = nc.dram_tensor("ALR", [128, KD * N], F32, kind="ExternalInput").ap()
    BDT = nc.dram_tensor("BDT", [128, KD], F32, kind="ExternalInput").ap()
    DV = nc.dram_tensor("DV", [128, KF], F32, kind="ExternalInput").ap()
    MK = nc.dram_tensor("MK", [128, 2], F32, kind="ExternalInput").ap()
    GWT = nc.dram_tensor("GWT", [D, II], F32, kind="ExternalInput").ap()
    UWT = nc.dram_tensor("UWT", [D, II], F32, kind="ExternalInput").ap()
    DWT = nc.dram_tensor("DWT", [II, D], F32, kind="ExternalInput").ap()
    GS = nc.dram_tensor("GS", [128, NIT], F32, kind="ExternalInput").ap()
    US = nc.dram_tensor("US", [128, NIT], F32, kind="ExternalInput").ap()
    DS = nc.dram_tensor("DS", [128, KF], F32, kind="ExternalInput").ap()
    SEL = nc.dram_tensor("SEL", [N, N * 128], F32, kind="ExternalInput").ap()
    IDN = nc.dram_tensor("IDN", [128, 128], F32, kind="ExternalInput").ap()
    OUT = nc.dram_tensor("OUT", [OWN, D], F32, kind="ExternalOutput").ap()
    DBGW = nc.dram_tensor("DBGW", [128, 1536], F32, kind="ExternalOutput").ap()

    CCI = [nc.dram_tensor(f"CCI{p}", [DJ, TC], BF16).ap() for p in range(2)]
    CCO = [nc.dram_tensor(f"CCO{p}", [D, TC], BF16).ap() for p in range(2)]
    X1D = nc.dram_tensor("X1D", [D, OWN], BF16).ap()
    GWB = nc.dram_tensor("GWB", [D, II], BF16).ap()
    UWB = nc.dram_tensor("UWB", [D, II], BF16).ap()
    DWB = nc.dram_tensor("DWB", [II, D], BF16).ap()

    P = Prog()
    LD = P.sem("LD")
    GD = P.sem("GD")
    PE = P.sem("PE")
    AC = P.sem("AC")
    VE = P.sem("VE")
    PO = P.sem("PO")
    CC = P.sem("CC")
    WB = P.sem("WB")
    CL = [P.sem(f"CL{i}") for i in range(2)]
    GL = [P.sem(f"GL{i}") for i in range(4)]
    UL = [P.sem(f"UL{i}") for i in range(4)]
    DL = [P.sem(f"DL{i}") for i in range(4)]
    YG0 = [P.sem(f"YG0{i}") for i in range(2)]
    YG1 = [P.sem(f"YG1{i}") for i in range(2)]
    XRs = [P.sem(f"XRs{i}") for i in range(2)]
    XS = [P.sem(f"XS{i}") for i in range(2)]
    XF = [P.sem(f"XF{i}") for i in range(2)]
    X1S = P.sem("X1S")
    OS = P.sem("OS")

    def ld2(dst_ap, src_ap, sem):
        return P.op("sync", lambda e, S, d=dst_ap, s=src_ap: e.dma_start(out=d, in_=s),
                    inc=sem, inc_amt=16)

    def ld(dst_ap, src_ap):
        return P.op("sync", lambda e, S, d=dst_ap, s=src_ap: e.dma_start(out=d, in_=s),
                    inc=LD, inc_amt=16)

    stTop = ExitStack()
    sb = lambda name, shp, dt=F32: stTop.enter_context(nc.sbuf_tensor(name, shp, dt))

    with stTop:
        s_wxt80 = [sb(f"wxa{k}", [128, 80], BF16) for k in range(KF)]
        s_wxt16 = [sb(f"wxb{k}", [128, 16], BF16) for k in range(KF)]
        s_wdt = sb("wdt", [RK, DJ], BF16)
        s_asb = sb("asb", [128, KD * N])
        s_bdt = sb("bdtc", [128, KD])
        s_dp1 = sb("dp1", [128, KF])
        s_mk = sb("mk", [128, 2])
        s_gs = sb("gsc", [128, NIT])
        s_us = sb("usc", [128, NIT])
        s_ds = sb("dsc", [128, KF])
        s_sel = sb("sel", [N, N * 128], BF16)
        s_idn = sb("idn", [128, 128])
        s_or = sb("onesr", [1, 128], BF16)
        s_oc = sb("onesc", [128, 1], BF16)
        s_state = sb("state", [128, KD * N])
        s_cst = sb("cst", [128, 2])
        s_row = [sb(f"row{i}", [1, TC]) for i in range(6)]
        s_rowb = [sb(f"rowb{i}", [1, TC], BF16) for i in range(2)]

        stBC = ExitStack()
        sb2 = lambda name, shp, dt=F32: stBC.enter_context(nc.sbuf_tensor(name, shp, dt))
        s_dtp = sb2("dtp", [RK, T], BF16)
        s_bb = sb2("bb16", [N, T], BF16)
        s_cb = sb2("cb16", [N, T], BF16)
        s_bfl = sb2("bfl", [128, N * TC], BF16)
        s_cfl = sb2("cfl", [128, N * TC], BF16)
        s_z = sb2("z", [128, TC])
        s_az = sb2("azb", [128, TC])
        s_e = sb2("eb", [128, TC])
        s_dtf = [sb2(f"dtf{p}", [128, TC]) for p in range(2)]
        s_xs = [sb2(f"xs{p}", [128, TC]) for p in range(2)]
        s_v = [sb2(f"vb{p}", [128, TC], BF16) for p in range(2)]
        s_abar = sb2("abar", [128, N * TC])
        s_u = sb2("ub", [128, N * TC], BF16)
        s_h = sb2("hb", [128, N * TC], BF16)

        s_y = sb2("y", [128, KD * TC], BF16)
        s_yg = [sb2(f"yg{p}", [128, TC], BF16) for p in range(2)]
        s_xr = [sb2(f"xr{p}", [128, TC]) for p in range(2)]
        s_r1 = [sb2(f"r1_{k}", [128, TC], BF16) for k in range(KF)]
        s_sqb = sb2("sqb", [128, TC], BF16)
        s_x1t = sb2("x1t", [128, TC], BF16)
        s_x1t2 = [sb2(f"x1t2{p}", [128, TC], BF16) for p in range(2)]
        s_cwf = sb2("cwf", [128, 1024])
        s_cwb = sb2("cwb", [128, 1024], BF16)
        a_xf = [s_abar[:, p * TC:(p + 1) * TC] for p in range(2)]
        a_xb = [s_u[:, p * TC:(p + 1) * TC] for p in range(2)]
        a_wxf = s_abar[:, 2 * TC:2 * TC + 96]
        a_self = s_abar[0:16, 3 * TC:3 * TC + N * 128]
        a_wdtf = s_abar[0:RK, 8 * TC:8 * TC + DJ]

        stBCp = ExitStack()
        ps2 = lambda name, shp: stBCp.enter_context(nc.psum_tensor(name, shp, F32))
        p_xp = ps2("p_xp", [80, TC])
        p_xpc = ps2("p_xpc", [16, TC])
        p_sel = [ps2(f"p_sel{i}", [128, TC]) for i in range(2)]
        p_dt = [ps2(f"p_dt{i}", [128, TC]) for i in range(2)]

        ld(a_wdtf, WDTT)
        ld(s_asb[:, :], ALR)
        ld(s_bdt[:, :], BDT)
        ld(s_dp1[:, :], DV)
        ld(s_mk[:, :], MK)
        ld(s_gs[:, :], GS)
        ld(s_us[:, :], US)
        ld(s_ds[:, :], DS)
        ld(a_self, SEL)
        ld(s_idn[:, :], IDN)
        consts_ld = P.cnt(LD)
        P.wait("vector", LD, consts_ld)
        P.op("vector", lambda e, S: e.tensor_copy(s_wdt[:, :], a_wdtf), inc=VE)
        P.op("vector", lambda e, S: e.tensor_copy(s_sel[:, :], a_self), inc=VE)
        P.op("vector", lambda e, S: e.memset(s_or[:, :], 1.0), inc=VE)
        P.op("vector", lambda e, S: e.memset(s_oc[:, :], 1.0), inc=VE)
        P.op("vector", lambda e, S: e.memset(s_state[:, :], 0.0), inc=VE)
        P.op("vector", lambda e, S: e.memset(s_cst[:, 0:1], 1.0), inc=VE)
        P.op("vector", lambda e, S: e.memset(s_cst[:, 1:2], EPS), inc=VE)
        P.op("vector", lambda e, S: e.tensor_scalar(s_dp1[:, :], s_dp1[:, :], 1.0, None, ALU.add), inc=VE)
        P.wait("scalar", LD, consts_ld)
        P.op("scalar", lambda e, S: e.activation(s_asb[:, :], s_asb[:, :], AF.Exp), inc=AC)
        P.wait("vector", AC)
        P.op("vector", lambda e, S: e.tensor_scalar(s_asb[:, :], s_asb[:, :], -1.0, None, ALU.mult), inc=VE)

        for k in range(KF):
            if k > 0:
                P.wait("sync", VE)
            ld(a_wxf, WXT[k * 128:(k + 1) * 128, :])
            P.wait("vector", LD)
            P.op("vector", lambda e, S, k=k: e.tensor_copy(s_wxt80[k][:, :], a_wxf[:, 0:80]), inc=VE)
            P.op("vector", lambda e, S, k=k: e.tensor_copy(s_wxt16[k][:, :], a_wxf[:, 80:96]), inc=VE)

        evac_ac = 0
        pe_par = [0, 0]
        ve_par = [0, 0]
        for c in range(NCH):
            for k in range(KF):
                i_ = c * KF + k
                par = i_ % 2
                P.wait("sync", AC, ve_par[par])
                xld = ld2(a_xf[par], XT[k * 128:(k + 1) * 128, c * TC:(c + 1) * TC], XF[par])
                P.wait("scalar", XF[par], xld)
                P.wait("scalar", PE, pe_par[par])
                P.op("scalar", lambda e, S, par=par: e.activation(a_xb[par], a_xf[par], AF.Copy), inc=AC)
                ve_par[par] = P.cnt(AC)
                P.wait("tensor", AC, ve_par[par])
                if k == 0 and c > 0:
                    P.wait("tensor", AC, evac_ac)
                P.op("tensor", lambda e, S, k=k, par=par: e.matmul(
                    p_xp[:, :], s_wxt80[k][:, :], a_xb[par],
                    start=(k == 0), stop=(k == KF - 1)), inc=PE)
                pe_par[par] = P.op("tensor", lambda e, S, k=k, par=par: e.matmul(
                    p_xpc[:, :], s_wxt16[k][:, :], a_xb[par],
                    start=(k == 0), stop=(k == KF - 1)), inc=PE)
            P.wait("scalar", PE)
            P.op("scalar", lambda e, S, c=c: e.copy(s_dtp[:, c * TC:(c + 1) * TC], p_xp[0:RK, :]), inc=AC)
            P.op("scalar", lambda e, S, c=c: e.activation(
                s_bb[:, c * TC:(c + 1) * TC], p_xp[64:80, :], AF.Copy), inc=AC)
            P.op("scalar", lambda e, S, c=c: e.activation(
                s_cb[:, c * TC:(c + 1) * TC], p_xpc[:, :], AF.Copy), inc=AC)
            evac_ac = P.cnt(AC)

        cache_blocks = []
        for k2 in range(KF):
            for (WSRC, WCACHE) in ((GWT, GWB), (UWT, UWB)):
                for c0 in range(0, II, 1024):
                    cache_blocks.append((WSRC, WCACHE, k2 * 128, c0, min(1024, II - c0)))
        for ki in range(NIT):
            for c0 in (0, 1024):
                cache_blocks.append((DWT, DWB, ki * 128, c0, 1024))
        cb_per_chunk = (len(cache_blocks) + NCH - 1) // NCH
        cb_idx = 0
        cb_cast_ac = [0, 0]
        cb_wb = [0, 0]

        ln_evac_done_ve = 0
        ln_stats_done_ac = 0
        cco_loads_done = (0, 0, 0, 0)
        last_u_po = 0
        last_tmp_ve = 0
        prev_chunk_dtf = 0
        x1s_cnt = [0, 0]
        ln_evac_done_po = 0
        ln_evac_done_ac = 0
        tts_done = [0, 0]
        u_war_ve = [0, 0]
        ac_dt = [0, 0]
        nch_run = 0 if upto < 2 else (1 if upto == 2 else (2 if upto == 3 else NCH))
        for c in range(nch_run):
            cpar = c % 2
            P.wait("scalar", PO, last_u_po)
            P.wait("scalar", VE, last_tmp_ve)
            P.wait("tensor", AC, evac_ac)
            bank_evac = [P.cnt(AC)] * 2
            bfl_h = [0, 0]
            cfl_h = [0, 0]

            def emit_sel(src, dst, hh_, c=c):
                for n8 in range(8):
                    n = hh_ * 8 + n8
                    bank = n % 2
                    P.wait("tensor", AC, bank_evac[bank])
                    if c > 0 and hh_ == 0 and n8 < 2 and src is s_bb:
                        P.wait("tensor", AC, ln_stats_done_ac)
                    P.op("tensor", lambda e, S, n=n, bank=bank, c=c, src=src: e.matmul(
                        p_sel[bank][:, :], s_sel[:, n * 128:(n + 1) * 128],
                        src[:, c * TC:(c + 1) * TC], start=True, stop=True), inc=PE)
                    P.wait("scalar", PE)
                    P.op("scalar", lambda e, S, n=n, bank=bank, dst=dst: e.activation(
                        dst[:, n * TC:(n + 1) * TC], p_sel[bank][:, :], AF.Copy), inc=AC)
                    bank_evac[bank] = P.cnt(AC)

            emit_sel(s_bb, s_bfl, 0)
            bfl_h[0] = P.cnt(AC)

            dtfv = {}
            vpo = {}
            upo = {}

            def emit_pre(k, c=c):
                kpar = k % 2
                if c * KD + k >= 2:
                    P.wait("sync", PO)
                xs_ld = ld2(s_xs[kpar][:, :], XSC[k * 128:(k + 1) * 128, c * TC:(c + 1) * TC], XS[kpar])
                P.wait("tensor", AC, ac_dt[kpar])
                if c > 0 and k < 2:
                    P.wait("tensor", AC, ln_evac_done_ac)
                P.op("tensor", lambda e, S, k=k, c=c, kpar=kpar: e.matmul(
                    p_dt[kpar][:, :], s_wdt[:, k * 128:(k + 1) * 128],
                    s_dtp[:, c * TC:(c + 1) * TC], start=True, stop=True), inc=PE)
                dtmm_pe = P.cnt(PE)
                P.wait("scalar", VE, dtfv.get(k - 1, prev_chunk_dtf))
                P.wait("scalar", PE, dtmm_pe)
                P.op("scalar", lambda e, S, k=k, kpar=kpar: e.activation(
                    s_z[:, :], p_dt[kpar][:, :], AF.Identity, bias=s_bdt[:, k:k + 1]), inc=AC)
                ac_dt[kpar] = P.cnt(AC)
                P.wait("scalar", AC)
                P.op("scalar", lambda e, S: e.activation(s_az[:, :], s_z[:, :], AF.Abs), inc=AC)
                P.wait("scalar", AC)
                P.op("scalar", lambda e, S: e.activation(s_e[:, :], s_az[:, :], AF.Exp, scale=-1.0), inc=AC)
                P.wait("scalar", AC)
                P.op("scalar", lambda e, S: e.activation(s_az[:, :], s_e[:, :], AF.Ln, bias=s_cst[:, 0:1]), inc=AC)
                P.wait("scalar", AC)
                P.op("scalar", lambda e, S: e.activation(s_e[:, :], s_z[:, :], AF.Relu), inc=AC)
                sp_ac = P.cnt(AC)
                P.wait("vector", AC, sp_ac)
                P.op("vector", lambda e, S, kpar=kpar: e.tensor_tensor(
                    s_dtf[kpar][:, :], s_e[:, :], s_az[:, :], ALU.add), inc=VE)
                dtfv[k] = P.cnt(VE)
                P.wait("gpsimd", VE, dtfv[k])
                P.wait("gpsimd", XS[kpar], xs_ld)
                P.wait("gpsimd", PO, upo.get((k - 2, 1), 0))
                P.op("gpsimd", lambda e, S, kpar=kpar: e.tensor_tensor(
                    s_v[kpar][:, :], s_dtf[kpar][:, :], s_xs[kpar][:, :], ALU.mult), inc=PO)
                vpo[k] = P.cnt(PO)

            HF = N * TC // 2

            def emit_half(k, hh_):
                kpar = k % 2
                ab = s_abar[:, hh_ * HF:(hh_ + 1) * HF]
                uh = s_u[:, hh_ * HF:(hh_ + 1) * HF]
                hbuf = s_h[:, hh_ * HF:(hh_ + 1) * HF]
                P.wait("scalar", VE, max(dtfv[k], tts_done[hh_]))
                for n8 in range(8):
                    n = hh_ * 8 + n8
                    P.op("scalar", lambda e, S, k=k, n=n, n8=n8, ab=ab, kpar=kpar: e.activation(
                        ab[:, n8 * TC:(n8 + 1) * TC], s_dtf[kpar][:, :], AF.Exp,
                        scale=s_asb[:, k * N + n:k * N + n + 1]), inc=AC)
                abar_ac = P.cnt(AC)
                P.wait("gpsimd", PO, vpo[k])
                P.wait("gpsimd", AC, bfl_h[hh_])
                P.wait("gpsimd", VE, u_war_ve[hh_])
                upo[(k, hh_)] = P.op("gpsimd", lambda e, S, uh=uh, hh_=hh_, kpar=kpar: e.tensor_tensor(
                    uh.rearrange("p (n t) -> p n t", n=8),
                    s_v[kpar][:, :].unsqueeze(1).broadcast_to((128, 8, TC)),
                    s_bfl[:, hh_ * HF:(hh_ + 1) * HF].rearrange("p (n t) -> p n t", n=8),
                    ALU.mult), inc=PO)
                P.wait("vector", AC, abar_ac)
                P.wait("vector", PO, upo[(k, hh_)])
                for n8 in range(8):
                    n = hh_ * 8 + n8
                    P.op("vector", lambda e, S, k=k, n=n, n8=n8, ab=ab, uh=uh, hbuf=hbuf: e.tensor_tensor_scan(
                        hbuf[:, n8 * TC:(n8 + 1) * TC], ab[:, n8 * TC:(n8 + 1) * TC],
                        uh[:, n8 * TC:(n8 + 1) * TC], s_state[:, k * N + n:k * N + n + 1],
                        ALU.mult, ALU.add), inc=VE)
                P.wait("vector", VE)
                P.op("vector", lambda e, S, k=k, hh_=hh_, hbuf=hbuf: e.tensor_copy(
                    s_state[:, k * N + hh_ * 8:k * N + hh_ * 8 + 8],
                    hbuf.rearrange("p (n t) -> p n t", n=8)[:, :, TC - 1:TC].squeeze(2)), inc=VE)
                tts_done[hh_] = P.cnt(VE)
                P.wait("vector", AC, cfl_h[hh_])
                P.op("vector", lambda e, S, uh=uh, hbuf=hbuf, hh_=hh_: e.tensor_tensor(
                    uh, hbuf, s_cfl[:, hh_ * HF:(hh_ + 1) * HF], ALU.mult), inc=VE)
                P.op("vector", lambda e, S, uh=uh, hbuf=hbuf: e.tensor_tensor(
                    hbuf[:, 0:2048], uh[:, 0:2048], uh[:, 2048:4096], ALU.add), inc=VE)
                u_war_ve[hh_] = P.cnt(VE)
                P.op("vector", lambda e, S, hbuf=hbuf: e.tensor_tensor(
                    hbuf[:, 2048:3072], hbuf[:, 0:1024], hbuf[:, 1024:2048], ALU.add), inc=VE)
                P.op("vector", lambda e, S, hbuf=hbuf: e.tensor_tensor(
                    hbuf[:, 3072:3584], hbuf[:, 2048:2560], hbuf[:, 2560:3072], ALU.add), inc=VE)

            def emit_cache_blocks(count):
                nonlocal_ns = emit_cache_blocks
                for _ in range(count):
                    if nonlocal_ns.idx >= len(cache_blocks):
                        return
                    WSRC, WCACHE, r0, c0, cw = cache_blocks[nonlocal_ns.idx]
                    nonlocal_ns.idx += 1
                    P.wait("sync", AC, cb_cast_ac[0])
                    cl = P.op("sync", lambda e, S, WSRC=WSRC, r0=r0, c0=c0, cw=cw: e.dma_start(
                        out=s_cwf[:, 0:cw], in_=WSRC[r0:r0 + 128, c0:c0 + cw]),
                        inc=CL[0], inc_amt=16)
                    P.wait("scalar", CL[0], cl)
                    P.wait("scalar", WB, cb_wb[0])
                    P.op("scalar", lambda e, S, cw=cw: e.activation(
                        s_cwb[:, 0:cw], s_cwf[:, 0:cw], AF.Copy), inc=AC)
                    cb_cast_ac[0] = P.cnt(AC)
                    P.wait("sync", AC, cb_cast_ac[0])
                    cb_wb[0] = P.op("sync", lambda e, S, WCACHE=WCACHE, r0=r0, c0=c0, cw=cw: e.dma_start(
                        out=WCACHE[r0:r0 + 128, c0:c0 + cw], in_=s_cwb[:, 0:cw]),
                        inc=WB, inc_amt=16)
            emit_cache_blocks.idx = cb_idx

            emit_pre(0)
            emit_sel(s_bb, s_bfl, 1)
            bfl_h[1] = P.cnt(AC)
            emit_sel(s_cb, s_cfl, 0)
            cfl_h[0] = P.cnt(AC)
            for k in range(KD):
                emit_half(k, 0)
                if k == 0:
                    emit_sel(s_cb, s_cfl, 1)
                    cfl_h[1] = P.cnt(AC)
                emit_half(k, 1)
                P.op("vector", lambda e, S, k=k, HF=HF: e.tensor_tensor(
                    s_y[:, k * TC:(k + 1) * TC], s_h[:, 3072:3584],
                    s_h[:, HF + 3072:HF + 3584], ALU.add), inc=VE)
                if k + 1 < KD:
                    emit_pre(k + 1)
                emit_cache_blocks(cb_per_chunk // KD + (1 if k < cb_per_chunk % KD else 0))
            cb_idx = emit_cache_blocks.idx
            P.wait("vector", VE)
            prev_chunk_dtf = dtfv[KD - 1]
            last_u_po = upo[(KD - 1, 1)]
            last_tmp_ve = u_war_ve[1]
            y_ve = P.cnt(VE)

            P.wait("gpsimd", VE, y_ve)
            if c >= 2:
                P.wait("gpsimd", CC, c - 1)
                for sm, v in zip(("YG00", "YG01", "YG10", "YG11"), cco_loads_done):
                    P.wait("gpsimd", sm, v)
            for k in range(KD):
                P.op("gpsimd", lambda e, S, k=k, cpar=cpar: e.dma_start(
                    out=CCI[cpar][k * 128:(k + 1) * 128, :],
                    in_=s_y[:, k * TC:(k + 1) * TC]), inc=GD, inc_amt=16)
            P.wait("gpsimd", GD)
            P.op("gpsimd", lambda e, S, cpar=cpar: e.collective_compute(
                "AllGather", ALU.bypass,
                replica_groups=[[0, 1], [2, 3], [4, 5], [6, 7]],
                ins=[CCI[cpar].opt()], outs=[CCO[cpar].opt()]), inc=CC)
            cc_done = P.cnt(CC)
            P.wait("vector", GD)

            if c % 2 == 1:
                s_slot = c // 2
                a_ya = [s_abar[:, p * TC:(p + 1) * TC] for p in range(2)]
                a_yo = [s_abar[:, (2 + p) * TC:(3 + p) * TC] for p in range(2)]
                a_sx = [s_abar[:, (4 + p) * TC:(5 + p) * TC] for p in range(2)]
                a_yg1 = [s_u[:, p * TC:(p + 1) * TC] for p in range(2)]
                P.wait("sync", CC, cc_done)
                ya_ve = [0, 0]
                yo_ve = [0, 0]
                r1_ve = [0, 0]
                sx_ac = [0, 0]
                for k2 in range(KF):
                    par = k2 % 2
                    P.wait("sync", VE, ya_ve[par])
                    yg_ld = ld2(s_yg[par][:, :], CCO[0][k2 * 128:(k2 + 1) * 128, :], YG0[par])
                    P.wait("sync", VE, yo_ve[par])
                    yg2_ld = ld2(a_yg1[par], CCO[1][k2 * 128:(k2 + 1) * 128, :], YG1[par])
                    P.wait("sync", VE, sx_ac[par])
                    xr_ld = ld2(s_xr[par][:, :], XOWN[k2 * 128:(k2 + 1) * 128, s_slot * TC:(s_slot + 1) * TC], XRs[par])
                    P.wait("vector", YG0[par], yg_ld)
                    P.op("vector", lambda e, S, par=par: e.tensor_scalar(
                        a_ya[par], s_yg[par][:, :], s_mk[:, 0:1], None, ALU.mult), inc=VE)
                    ya_ve[par] = P.cnt(VE)
                    P.wait("vector", YG1[par], yg2_ld)
                    P.op("vector", lambda e, S, par=par: e.scalar_tensor_tensor(
                        a_yo[par], a_yg1[par], s_mk[:, 1:2], a_ya[par],
                        ALU.mult, ALU.add), inc=VE)
                    yo_ve[par] = P.cnt(VE)
                    P.wait("vector", XRs[par], xr_ld)
                    P.op("vector", lambda e, S, k2=k2, par=par: e.tensor_scalar(
                        a_sx[par], s_xr[par][:, :], s_dp1[:, k2:k2 + 1], None, ALU.mult), inc=VE)
                    sx_ac[par] = P.cnt(VE)
                    P.op("vector", lambda e, S, k2=k2, par=par: e.tensor_tensor(
                        s_r1[k2][:, :], a_sx[par], a_yo[par], ALU.add), inc=VE)
                    r1_ve[par] = P.cnt(VE)
                    P.wait("scalar", VE, r1_ve[par])
                    P.wait("scalar", PE)
                    P.op("scalar", lambda e, S, k2=k2: e.activation(
                        s_sqb[:, :], s_r1[k2][:, :], AF.Square), inc=AC)
                    cast_ac = P.cnt(AC)
                    P.wait("tensor", AC, cast_ac)
                    P.op("tensor", lambda e, S, k2=k2: e.matmul(
                        p_sel[0][0:1, :], s_oc[:, :], s_r1[k2][:, :],
                        start=(k2 == 0), stop=(k2 == KF - 1)), inc=PE)
                    P.op("tensor", lambda e, S, k2=k2: e.matmul(
                        p_sel[1][0:1, :], s_oc[:, :], s_sqb[:, :],
                        start=(k2 == 0), stop=(k2 == KF - 1)), inc=PE)
                stats_pe = P.cnt(PE)
                P.wait("scalar", PE, stats_pe)
                P.op("scalar", lambda e, S: e.copy(s_row[0][:, :], p_sel[0][0:1, :]), inc=AC)
                P.wait("scalar", AC)
                P.op("scalar", lambda e, S: e.copy(s_row[1][:, :], p_sel[1][0:1, :]), inc=AC)
                ln_stats_done_ac = P.cnt(AC)
                P.wait("vector", AC, ln_stats_done_ac)
                P.op("vector", lambda e, S: e.tensor_scalar(s_row[2][:, :], s_row[0][:, :], 1.0 / D, None, ALU.mult), inc=VE)
                P.wait("vector", VE)
                P.op("vector", lambda e, S: e.tensor_scalar(s_row[3][:, :], s_row[1][:, :], 1.0 / D, None, ALU.mult), inc=VE)
                P.wait("vector", VE)
                P.op("vector", lambda e, S: e.tensor_tensor(s_row[4][:, :], s_row[2][:, :], s_row[2][:, :], ALU.mult), inc=VE)
                P.wait("vector", VE)
                P.op("vector", lambda e, S: e.tensor_tensor(s_row[3][:, :], s_row[3][:, :], s_row[4][:, :], ALU.subtract), inc=VE)
                var_ve = P.cnt(VE)
                P.wait("scalar", VE, var_ve)
                P.op("scalar", lambda e, S: e.activation(s_row[4][:, :], s_row[3][:, :], AF.Ln, bias=s_cst[0:1, 1:2]), inc=AC)
                P.wait("scalar", AC)
                P.op("scalar", lambda e, S: e.activation(s_row[4][:, :], s_row[4][:, :], AF.Exp, scale=-0.5), inc=AC)
                rstd_ac = P.cnt(AC)
                P.wait("vector", AC, rstd_ac)
                P.op("vector", lambda e, S: e.tensor_tensor(s_row[5][:, :], s_row[2][:, :], s_row[4][:, :], ALU.mult), inc=VE)
                P.wait("vector", VE)
                P.op("vector", lambda e, S: e.tensor_scalar(s_row[5][:, :], s_row[5][:, :], -1.0, None, ALU.mult), inc=VE)
                P.wait("vector", VE)
                P.op("vector", lambda e, S: e.tensor_copy(s_rowb[0][:, :], s_row[4][:, :]), inc=VE)
                P.wait("vector", VE)
                P.op("vector", lambda e, S: e.tensor_copy(s_rowb[1][:, :], s_row[5][:, :]), inc=VE)
                rowb_ve = P.cnt(VE)
                P.wait("tensor", VE, rowb_ve)
                P.wait("tensor", AC)
                P.op("tensor", lambda e, S: e.matmul(
                    p_dt[0][:, :], s_or[:, :], s_rowb[0][:, :], start=True, stop=True), inc=PE)
                P.op("tensor", lambda e, S: e.matmul(
                    p_dt[1][:, :], s_or[:, :], s_rowb[1][:, :], start=True, stop=True), inc=PE)
                bc_pe = P.cnt(PE)
                P.wait("scalar", PE, bc_pe)
                P.op("scalar", lambda e, S: e.activation(
                    s_abar[:, 6 * TC:7 * TC], p_dt[0][:, :], AF.Copy), inc=AC)
                P.op("scalar", lambda e, S: e.activation(
                    s_abar[:, 7 * TC:8 * TC], p_dt[1][:, :], AF.Copy), inc=AC)
                bc_ac = P.cnt(AC)
                P.wait("gpsimd", AC, bc_ac)
                for k2 in range(KF):
                    ppar = k2 % 2
                    P.op("gpsimd", lambda e, S, k2=k2, ppar=ppar: e.tensor_tensor(
                        s_abar[:, ppar * TC:(ppar + 1) * TC], s_r1[k2][:, :],
                        s_abar[:, 6 * TC:7 * TC], ALU.mult), inc=PO)
                    x1m_po = P.cnt(PO)
                    P.wait("gpsimd", X1S, 0)
                    P.op("gpsimd", lambda e, S, k2=k2, ppar=ppar: e.tensor_tensor(
                        s_x1t[:, :] if False else s_abar[:, (2 + ppar) * TC:(3 + ppar) * TC],
                        s_abar[:, ppar * TC:(ppar + 1) * TC],
                        s_abar[:, 7 * TC:8 * TC], ALU.add), inc=PO)
                    x1a_po = P.cnt(PO)
                    P.wait("scalar", PO, x1a_po)
                    P.wait("scalar", X1S, x1s_cnt[ppar])
                    P.op("scalar", lambda e, S, k2=k2, ppar=ppar: e.activation(
                        s_x1t[:, ppar * 0:TC] if False else s_x1t2[ppar][:, :],
                        s_abar[:, (2 + ppar) * TC:(3 + ppar) * TC], AF.Copy), inc=AC)
                    x1c_ac = P.cnt(AC)
                    P.wait("sync", AC, x1c_ac)
                    x1s_cnt[ppar] = P.op("sync", lambda e, S, k2=k2, s_slot=s_slot, ppar=ppar: e.dma_start(
                        out=X1D[k2 * 128:(k2 + 1) * 128, s_slot * TC:(s_slot + 1) * TC],
                        in_=s_x1t2[ppar][:, :]), inc=X1S, inc_amt=16)
                ln_evac_done_ve = P.cnt(VE)
                ln_evac_done_po = P.cnt(PO)
                ln_evac_done_ac = P.cnt(AC)
                cco_loads_done = (P.cnt("YG00"), P.cnt("YG01"), P.cnt("YG10"), P.cnt("YG11"))

        assert cb_idx >= len(cache_blocks) - 0 or True
        scan_done = {s: P.cnt(s) for s in ("LD", "GD", "PE", "AC", "VE", "PO", "CC")}

        stBC.close()
        stBCp.close()

        sbD = lambda name, shp, dt=F32: stTop.enter_context(nc.sbuf_tensor(name, shp, dt))
        s_x1m = [sbD(f"x1m{k}", [128, MW], BF16) for k in range(KF)]
        s_gsb = [sbD(f"gsb{p}", [128, 256], BF16) for p in range(4)]
        s_usb = [sbD(f"usb{p}", [128, 256], BF16) for p in range(4)]
        s_dsb = [sbD(f"dsb{p}", [128, 1024], BF16) for p in range(4)]
        s_sg = [sbD(f"sg{p}", [128, TC], BF16) for p in range(2)]
        s_uu = [sbD(f"uu{p}", [128, TC], BF16) for p in range(2)]
        s_hdn = sbD("hdn", [128, NIT * MW], BF16)
        s_r2 = [sbD(f"r2_{k}", [128, TC], BF16) for k in range(KF)]
        s_r2a = sbD("r2a", [128, TC])
        s_otm = [sbD(f"otm{p}", [128, TC]) for p in range(2)]
        s_ost = [sbD(f"ost{p}", [128, D // 2]) for p in range(4)]
        p_mm = [stTop.enter_context(nc.psum_tensor(f"p_mm{i}", [128, TC], F32))
                for i in range(8)]

        for eng in ENGINES:
            for sm in ("PE", "AC", "VE", "PO"):
                P.wait(eng, sm, scan_done[sm])
        P.wait("sync", WB)
        P.wait("scalar", WB)

        gl = [0] * 4
        ul = [0] * 4
        dl = [0] * 4
        pe_gsb = [0] * 4
        pe_usb = [0] * 4
        pe_dsb = [0] * 4
        evac_bank = [0] * 8
        evac_bank_eng = ["AC"] * 8
        ost_os = 0
        for m in range(NMEG):
            if m > 0:
                P.wait("sync", PE)
                P.wait("sync", VE)
                P.wait("gpsimd", PE)
            for k2 in range(KF):
                ld(s_x1m[k2][:, :], X1D[k2 * 128:(k2 + 1) * 128, m * MW:(m + 1) * MW])
            x1_ld = P.cnt(LD)

            for ig in range(NIT // 2):
                for k2 in range(KF):
                    par = k2 % 4
                    P.wait("sync", PE, pe_gsb[par])
                    gl[par] = P.op("sync", lambda e, S, k2=k2, ig=ig, par=par: e.dma_start(
                        out=s_gsb[par][:, :],
                        in_=GWB[k2 * 128:(k2 + 1) * 128, ig * 256:(ig + 1) * 256]),
                        inc=GL[par], inc_amt=16)
                    P.wait("scalar", PE, pe_usb[par])
                    ul[par] = P.op("scalar", lambda e, S, k2=k2, ig=ig, par=par: e.dma_start(
                        out=s_usb[par][:, :],
                        in_=UWB[k2 * 128:(k2 + 1) * 128, ig * 256:(ig + 1) * 256]),
                        inc=UL[par], inc_amt=16)
                    P.wait("tensor", GL[par], gl[par])
                    P.wait("tensor", UL[par], ul[par])
                    if k2 == 0:
                        P.wait("tensor", LD, x1_ld)
                        for bank in range(8):
                            P.wait("tensor", evac_bank_eng[bank], evac_bank[bank])
                    for w_i in range(2):
                        src = s_gsb[par] if w_i == 0 else s_usb[par]
                        for ii in range(2):
                            for ng in range(2):
                                bank = w_i * 4 + ii * 2 + ng
                                P.op("tensor", lambda e, S, k2=k2, ii=ii, ng=ng, bank=bank, src=src: e.matmul(
                                    p_mm[bank][:, :], src[:, ii * 128:(ii + 1) * 128],
                                    s_x1m[k2][:, ng * TC:(ng + 1) * TC],
                                    start=(k2 == 0), stop=(k2 == KF - 1)), inc=PE)
                    pe_gsb[par] = P.cnt(PE)
                    pe_usb[par] = P.cnt(PE)
                mm_pe = P.cnt(PE)
                for ng in range(2):
                    for ii in range(2):
                        i = ig * 2 + ii
                        gbank = ii * 2 + ng
                        ubank = 4 + ii * 2 + ng
                        P.wait("scalar", PE, mm_pe)
                        P.wait("scalar", PO)
                        P.op("scalar", lambda e, S, i=i, ii=ii, gbank=gbank: e.activation(
                            s_sg[ii][:, :], p_mm[gbank][:, :], AF.Silu,
                            scale=s_gs[:, i:i + 1]), inc=AC)
                        evac_bank[gbank] = P.cnt(AC)
                        evac_bank_eng[gbank] = "AC"
                        P.wait("vector", PE, mm_pe)
                        P.wait("vector", PO)
                        P.op("vector", lambda e, S, i=i, ii=ii, ubank=ubank: e.tensor_scalar(
                            s_uu[ii][:, :], p_mm[ubank][:, :], s_us[:, i:i + 1],
                            None, ALU.mult), inc=VE)
                        evac_bank[ubank] = P.cnt(VE)
                        evac_bank_eng[ubank] = "VE"
                    sg_ac = P.cnt(AC)
                    uu_ve = P.cnt(VE)
                    P.wait("gpsimd", AC, sg_ac)
                    P.wait("gpsimd", VE, uu_ve)
                    for ii in range(2):
                        i = ig * 2 + ii
                        P.op("gpsimd", lambda e, S, i=i, ng=ng, ii=ii: e.tensor_tensor(
                            s_hdn[:, i * MW + ng * TC:i * MW + (ng + 1) * TC],
                            s_sg[ii][:, :], s_uu[ii][:, :], ALU.mult), inc=PO)
            hdn_po = P.cnt(PO)

            for ts in range(MW // TC):
                for dg2 in range(2):
                    for ki in range(NIT):
                        par = ki % 4
                        q = "sync" if ki % 2 == 0 else "scalar"
                        P.wait(q, PE, pe_dsb[par])
                        dl[par] = P.op(q, lambda e, S, ki=ki, dg2=dg2, par=par: e.dma_start(
                            out=s_dsb[par][:, :],
                            in_=DWB[ki * 128:(ki + 1) * 128, dg2 * 1024:(dg2 + 1) * 1024]),
                            inc=DL[par], inc_amt=16)
                        P.wait("tensor", DL[par], dl[par])
                        if ki == 0:
                            P.wait("tensor", PO, hdn_po)
                            for bank in range(8):
                                P.wait("tensor", evac_bank_eng[bank], evac_bank[bank])
                        for dd in range(8):
                            P.op("tensor", lambda e, S, ki=ki, dd=dd, ts=ts, par=par: e.matmul(
                                p_mm[dd][:, :], s_dsb[par][:, dd * 128:(dd + 1) * 128],
                                s_hdn[:, ki * MW + ts * TC:ki * MW + (ts + 1) * TC],
                                start=(ki == 0), stop=(ki == NIT - 1)), inc=PE)
                        pe_dsb[par] = P.cnt(PE)
                    dwn_pe = P.cnt(PE)
                    for dd in range(8):
                        dk = dg2 * 8 + dd
                        P.wait("scalar", PE, dwn_pe)
                        P.wait("scalar", VE)
                        P.op("scalar", lambda e, S, dk=dk, dd=dd: e.activation(
                            s_r2a[:, :], p_mm[dd][:, :], AF.Identity,
                            scale=s_ds[:, dk:dk + 1]), inc=AC)
                        r2a_ac = P.cnt(AC)
                        evac_bank[dd] = r2a_ac
                        evac_bank_eng[dd] = "AC"
                        P.wait("vector", AC, r2a_ac)
                        P.op("vector", lambda e, S, dk=dk, ts=ts: e.tensor_tensor(
                            s_r2[dk][:, :], s_r2a[:, :],
                            s_x1m[dk][:, ts * TC:(ts + 1) * TC], ALU.add), inc=VE)
                r2_ve = P.cnt(VE)

                s_sq2 = s_uu[1]
                for dk in range(KF):
                    P.wait("scalar", VE, r2_ve)
                    P.wait("scalar", PE)
                    P.op("scalar", lambda e, S, dk=dk: e.activation(
                        s_sq2[:, :], s_r2[dk][:, :], AF.Square), inc=AC)
                    cast_ac = P.cnt(AC)
                    P.wait("tensor", AC, cast_ac)
                    if dk == 0:
                        P.wait("tensor", evac_bank_eng[0], evac_bank[0])
                        P.wait("tensor", evac_bank_eng[1], evac_bank[1])
                    P.op("tensor", lambda e, S, dk=dk: e.matmul(
                        p_mm[0][0:1, :], s_oc[:, :], s_r2[dk][:, :],
                        start=(dk == 0), stop=(dk == KF - 1)), inc=PE)
                    P.op("tensor", lambda e, S, dk=dk: e.matmul(
                        p_mm[1][0:1, :], s_oc[:, :], s_sq2[:, :],
                        start=(dk == 0), stop=(dk == KF - 1)), inc=PE)
                st2_pe = P.cnt(PE)
                P.wait("scalar", PE, st2_pe)
                P.op("scalar", lambda e, S: e.copy(s_row[0][:, :], p_mm[0][0:1, :]), inc=AC)
                P.wait("scalar", AC)
                P.op("scalar", lambda e, S: e.copy(s_row[1][:, :], p_mm[1][0:1, :]), inc=AC)
                st2_ac = P.cnt(AC)
                evac_bank[0] = st2_ac
                evac_bank_eng[0] = "AC"
                evac_bank[1] = st2_ac
                evac_bank_eng[1] = "AC"
                P.wait("vector", AC, st2_ac)
                P.op("vector", lambda e, S: e.tensor_scalar(s_row[2][:, :], s_row[0][:, :], 1.0 / D, None, ALU.mult), inc=VE)
                P.wait("vector", VE)
                P.op("vector", lambda e, S: e.tensor_scalar(s_row[3][:, :], s_row[1][:, :], 1.0 / D, None, ALU.mult), inc=VE)
                P.wait("vector", VE)
                P.op("vector", lambda e, S: e.tensor_tensor(s_row[4][:, :], s_row[2][:, :], s_row[2][:, :], ALU.mult), inc=VE)
                P.wait("vector", VE)
                P.op("vector", lambda e, S: e.tensor_tensor(s_row[3][:, :], s_row[3][:, :], s_row[4][:, :], ALU.subtract), inc=VE)
                var_ve = P.cnt(VE)
                P.wait("scalar", VE, var_ve)
                P.op("scalar", lambda e, S: e.activation(s_row[4][:, :], s_row[3][:, :], AF.Ln, bias=s_cst[0:1, 1:2]), inc=AC)
                P.wait("scalar", AC)
                P.op("scalar", lambda e, S: e.activation(s_row[4][:, :], s_row[4][:, :], AF.Exp, scale=-0.5), inc=AC)
                rstd_ac = P.cnt(AC)
                P.wait("vector", AC, rstd_ac)
                P.op("vector", lambda e, S: e.tensor_tensor(s_row[5][:, :], s_row[2][:, :], s_row[4][:, :], ALU.mult), inc=VE)
                P.wait("vector", VE)
                P.op("vector", lambda e, S: e.tensor_scalar(s_row[5][:, :], s_row[5][:, :], -1.0, None, ALU.mult), inc=VE)
                P.wait("vector", VE)
                P.op("vector", lambda e, S: e.tensor_copy(s_rowb[0][:, :], s_row[4][:, :]), inc=VE)
                P.wait("vector", VE)
                P.op("vector", lambda e, S: e.tensor_copy(s_rowb[1][:, :], s_row[5][:, :]), inc=VE)
                rowb_ve = P.cnt(VE)
                P.wait("tensor", VE, rowb_ve)
                P.wait("tensor", evac_bank_eng[2], evac_bank[2])
                P.wait("tensor", evac_bank_eng[3], evac_bank[3])
                P.op("tensor", lambda e, S: e.matmul(
                    p_mm[2][:, :], s_or[:, :], s_rowb[0][:, :], start=True, stop=True), inc=PE)
                P.op("tensor", lambda e, S: e.matmul(
                    p_mm[3][:, :], s_or[:, :], s_rowb[1][:, :], start=True, stop=True), inc=PE)
                bc_pe = P.cnt(PE)

                P.wait("vector", PE, bc_pe)
                tr_bank = [0, 0]
                tr_bank_eng = [False, False]
                otm_tr_pe = [0, 0]
                for dh in range(2):
                    P.wait("vector", OS, ost_os)
                    for dk in range(dh * 8, dh * 8 + 8):
                        opar = dk % 2
                        P.wait("vector", PE, otm_tr_pe[opar])
                        P.op("vector", lambda e, S, dk=dk: e.tensor_tensor(
                            s_r2a[:, :], s_r2[dk][:, :], p_mm[2][:, :], ALU.mult), inc=VE)
                        P.op("vector", lambda e, S, dk=dk, opar=opar: e.tensor_tensor(
                            s_otm[opar][:, :], s_r2a[:, :], p_mm[3][:, :], ALU.add), inc=VE)
                        otm_ve = P.cnt(VE)
                        for tb in range(4):
                            tbank = 4 + (tb % 2)
                            tq = tb // 2
                            P.wait("tensor", VE, otm_ve)
                            P.wait("tensor", "VE" if tr_bank_eng[tbank - 4] else evac_bank_eng[tbank],
                                   tr_bank[tbank - 4] if tr_bank_eng[tbank - 4] else evac_bank[tbank])
                            P.op("tensor", lambda e, S, dk=dk, tb=tb, tbank=tbank, tq=tq, opar=opar: e.transpose(
                                p_mm[tbank][:, tq * 128:(tq + 1) * 128],
                                s_otm[opar][:, tb * 128:(tb + 1) * 128], s_idn[:, :]), inc=PE)
                            tr_pe = P.cnt(PE)
                            P.wait("vector", PE, tr_pe)
                            P.op("vector", lambda e, S, dk=dk, tb=tb, tbank=tbank, tq=tq: e.tensor_copy(
                                s_ost[tb][:, (dk % 8) * 128:((dk % 8) + 1) * 128],
                                p_mm[tbank][:, tq * 128:(tq + 1) * 128]), inc=VE)
                            tr_bank[tbank - 4] = P.cnt(VE)
                            tr_bank_eng[tbank - 4] = True
                        otm_tr_pe[opar] = P.cnt(PE)
                    ostv = P.cnt(VE)
                    P.wait("sync", VE, ostv)
                    for tb in range(4):
                        ost_os = P.op("sync", lambda e, S, m=m, ts=ts, tb=tb, dh=dh: e.dma_start(
                            out=OUT[m * MW + ts * TC + tb * 128:m * MW + ts * TC + (tb + 1) * 128,
                                    dh * (D // 2):(dh + 1) * (D // 2)],
                            in_=s_ost[tb][:, :]), inc=OS, inc_amt=16)
                ostv = P.cnt(VE)
                for bank in (2, 3, 4, 5):
                    evac_bank[bank] = ostv
                    evac_bank_eng[bank] = "VE"

        P.wait("sync", LD)
        P.wait("sync", OS)
        P.wait("sync", WB)

        sems_ctx = ExitStack()
        S = {}
        for name in ("LD", "GD", "PE", "AC", "VE", "PO", "CC", "WB",
                     "CL0", "CL1", "GL0", "GL1", "GL2", "GL3", "UL0", "UL1", "UL2", "UL3", "DL0", "DL1", "DL2", "DL3",
                     "YG00", "YG01", "YG10", "YG11", "XRs0", "XRs1", "XS0", "XS1", "XF0", "XF1", "X1S", "OS"):
            S[name] = sems_ctx.enter_context(nc.semaphore(name))
        with sems_ctx:
            with nc.Block() as block:
                @block.sync
                def _(sync):
                    for f in P.streams["sync"]:
                        f(sync, S)

                @block.tensor
                def _(tensor):
                    for f in P.streams["tensor"]:
                        f(tensor, S)

                @block.scalar
                def _(scalar):
                    for f in P.streams["scalar"]:
                        f(scalar, S)

                @block.vector
                def _(vector):
                    for f in P.streams["vector"]:
                        f(vector, S)

                @block.gpsimd
                def _(gpsimd):
                    for f in P.streams["gpsimd"]:
                        f(gpsimd, S)
    return nc


_NC_CACHE = None


def kernel(**inputs):
    global _NC_CACHE
    x = np.ascontiguousarray(np.asarray(inputs["L0_attn_x"], np.float32))
    A_log = np.ascontiguousarray(np.asarray(inputs["L0_attn_A_log"], np.float32))
    Dv = np.ascontiguousarray(np.asarray(inputs["L0_attn_D"], np.float32))
    Wx = np.ascontiguousarray(np.asarray(inputs["L0_attn_W_xproj"], np.float32))
    Wdt = np.ascontiguousarray(np.asarray(inputs["L0_attn_W_dt"], np.float32))
    bdt = np.ascontiguousarray(np.asarray(inputs["L0_attn_b_dt"], np.float32))
    gw = np.ascontiguousarray(np.asarray(inputs["L0_ffn_gate_w"], np.float32))
    uw = np.ascontiguousarray(np.asarray(inputs["L0_ffn_up_w"], np.float32))
    dw = np.ascontiguousarray(np.asarray(inputs["L0_ffn_down_w"], np.float32))
    gs = np.ascontiguousarray(np.asarray(inputs["L0_ffn_gate_s"], np.float32))
    us = np.ascontiguousarray(np.asarray(inputs["L0_ffn_up_s"], np.float32))
    ds = np.ascontiguousarray(np.asarray(inputs["L0_ffn_down_s"], np.float32))

    WXTa = np.ascontiguousarray(Wx.T)
    GWTa = np.ascontiguousarray(gw.T)
    UWTa = np.ascontiguousarray(uw.T)
    DWTa = np.ascontiguousarray(dw.T)
    GSa = np.ascontiguousarray(gs.reshape(NIT, 128).T)
    USa = np.ascontiguousarray(us.reshape(NIT, 128).T)
    DSa = np.ascontiguousarray(ds.reshape(KF, 128).T)
    DVa = np.ascontiguousarray(Dv.reshape(KF, 128).T)
    SELa = np.zeros((N, N * 128), np.float32)
    for n in range(N):
        SELa[n, n * 128:(n + 1) * 128] = 1.0
    IDNa = np.eye(128, dtype=np.float32)

    in_maps = []
    for core in range(8):
        b = core // 2
        j = core % 2
        xbT = np.ascontiguousarray(x[b].T)
        dj = slice(j * DJ, (j + 1) * DJ)
        own_cols = np.concatenate([np.arange(c * TC, (c + 1) * TC)
                                   for c in range(j, NCH, 2)])
        m = dict(
            XT=xbT,
            XSC=np.ascontiguousarray(xbT[dj, :]),
            XOWN=np.ascontiguousarray(xbT[:, own_cols]),
            WXT=WXTa,
            WDTT=np.ascontiguousarray(Wdt.T[:, dj]),
            ALR=np.ascontiguousarray(
                A_log[dj].reshape(KD, 128, N).transpose(1, 0, 2).reshape(128, KD * N)),
            BDT=np.ascontiguousarray(bdt[dj].reshape(KD, 128).T),
            DV=DVa,
            MK=np.ascontiguousarray(
                np.repeat(np.array([[1.0 - j, float(j)]], np.float32), 128, axis=0)),
            GWT=GWTa, UWT=UWTa, DWT=DWTa,
            GS=GSa, US=USa, DS=DSa,
            SEL=SELa, IDN=IDNa,
        )
        in_maps.append(m)

    if _NC_CACHE is None:
        _NC_CACHE = build()
    res = run_bass_kernel_spmd(_NC_CACHE, in_maps, core_ids=list(range(8)))
    kernel._LAST_RES = res

    out = np.empty((B, T, D), np.float32)
    for core in range(8):
        b = core // 2
        j = core % 2
        o = np.asarray(res.results[core]["OUT"], np.float32)
        for si, c in enumerate(range(j, NCH, 2)):
            out[b, c * TC:(c + 1) * TC, :] = o[si * TC:(si + 1) * TC, :]
    return out
